# version 03
# baseline (speedup 1.0000x reference)
import math
import numpy as np
import ml_dtypes
import concourse.bass as bass
import concourse.mybir as mybir
from concourse.bass_utils import run_bass_kernel_spmd

F32 = mybir.dt.float32
BF16 = mybir.dt.bfloat16
U8 = mybir.dt.uint8
I32 = mybir.dt.int32
AF = mybir.ActivationFunctionType
ALU = mybir.AluOpType

BFNP = ml_dtypes.bfloat16

NCORE = 8
T = 8192
D = 4096
DH = 128
TPC = T // NCORE
NTT = TPC // 128
NDC = D // 128
NQH = 4
NTCH = T // 512
GT = T // 128
SCALE = 1.0 / math.sqrt(DH)

ENGINES = ("sync", "scalar", "vector", "tensor", "gpsimd")


def build_MT(n=D):
    j = np.arange(n, dtype=np.float64)[:, None]
    M = np.zeros((n, n), dtype=np.float64)
    M[:, 0] = 1.0 / math.sqrt(n)
    max_f = (n - 1) // 2
    f = np.arange(1, max_f + 1, dtype=np.float64)[None, :]
    ang = 2.0 * math.pi * f * j / n
    M[:, 1:1 + 2 * max_f:2] = math.sqrt(2.0 / n) * np.cos(ang)
    M[:, 2:2 + 2 * max_f:2] = math.sqrt(2.0 / n) * np.sin(ang)
    M[:, n - 1] = ((-1.0) ** j[:, 0]) / math.sqrt(n)
    return np.ascontiguousarray(M.T)


def rope_tables():
    theta, factor, lo, hi, L0 = 500000.0, 32.0, 1.0, 4.0, 8192
    idx = np.arange(0, DH, 2, dtype=np.float64) / DH
    freq = 1.0 / (2.0 * math.pi) * theta ** (-idx)
    freq_low, freq_high = lo / L0, hi / L0
    freq_scaled = np.where(freq < freq_low, freq / factor, freq)
    smooth = np.clip((L0 * freq - lo) / (hi - lo), 0.0, 1.0)
    freq_smooth = (1.0 - smooth) * (freq / factor) + smooth * freq
    is_mid = (freq >= freq_low) & (freq <= freq_high)
    freq = np.where(is_mid, freq_smooth, freq_scaled)
    pos = np.arange(T, dtype=np.float64)
    phase = 2.0 * math.pi * pos[:, None] * freq[None, :]
    return np.cos(phase), np.sin(phase)


class Plan:

    def __init__(self, nc):
        self.nc = nc
        self.items = {e: [] for e in ENGINES}
        self.handles = {}
        self.counts = {}

    def sem(self, name):
        if name not in self.handles:
            self.handles[name] = self.nc.alloc_semaphore(name)
            self.counts[name] = 0
        return name

    def n(self, name):
        return self.counts[name]

    def emit(self, eng, fn, waits=(), inc=None, inc_amt=1):
        if inc is not None:
            self.sem(inc)
            self.counts[inc] += inc_amt
        ws = [(self.sem(s), int(v)) for (s, v) in waits if v > 0]
        self.items[eng].append((fn, ws, inc, inc_amt))

    def build(self, blk):
        for engname in ("gpsimd", "scalar", "vector", "tensor", "sync"):
            items = self.items[engname]
            if not items:
                continue

            def make_body(items=items):
                def body(e):
                    for fn, ws, inc, amt in items:
                        for s, v in ws:
                            e.wait_ge(self.handles[s], v)
                        instr = fn(e)
                        if inc is not None:
                            instr.then_inc(self.handles[inc], amt)
                return body
            getattr(blk, engname)(make_body())


def build_program(dbg=(), upto="p6"):
    nc = bass.Bass("TRN2", target_bir_lowering=False, debug=False)

    xc = nc.dram_tensor("xc", [TPC, D], F32, kind="ExternalInput").ap()
    wq_my = nc.dram_tensor("wq_my", [D, 512], F32, kind="ExternalInput").ap()
    wkv_my = nc.dram_tensor("wkv_my", [D, 256], F32, kind="ExternalInput").ap()
    wo = nc.dram_tensor("wo", [D, D], F32, kind="ExternalInput").ap()
    wbT = nc.dram_tensor("wbT", [D, D], F32, kind="ExternalInput").ap()
    mt = nc.dram_tensor("mt", [D, D], BF16, kind="ExternalInput").ap()
    enc01T = nc.dram_tensor("enc01T", [D, 40], F32, kind="ExternalInput").ap()
    c0r = nc.dram_tensor("c0r", [3, 128, 1024], F32, kind="ExternalInput").ap()
    c1r = nc.dram_tensor("c1r", [3, 128, 256], F32, kind="ExternalInput").ap()
    m0r = nc.dram_tensor("m0r", [3, 128, 1024], U8, kind="ExternalInput").ap()
    m1r = nc.dram_tensor("m1r", [3, 128, 256], U8, kind="ExternalInput").ap()
    bbt = nc.dram_tensor("bb", [1, D], F32, kind="ExternalInput").ap()
    cosT = nc.dram_tensor("cosT", [128, GT * 128], F32, kind="ExternalInput").ap()
    sinT = nc.dram_tensor("sinT", [128, GT * 128], F32, kind="ExternalInput").ap()
    tri = nc.dram_tensor("tri", [128, 128], BF16, kind="ExternalInput").ap()
    onesc = nc.dram_tensor("onesc", [128, 128], BF16, kind="ExternalInput").ap()
    identb = nc.dram_tensor("identb", [128, 128], BF16, kind="ExternalInput").ap()
    cofs = nc.dram_tensor("cofs", [1, 2], I32, kind="ExternalInput").ap()

    out = nc.dram_tensor("out", [TPC, D], F32, kind="ExternalOutput").ap()

    cin_xn = nc.dram_tensor("cin_xn", [D, TPC], BF16, kind="Internal").ap()
    cout_xn = nc.dram_tensor("cout_xn", [NCORE * D, TPC], BF16, kind="Internal", addr_space="Shared").ap()
    cin_c = nc.dram_tensor("cin_c", [512, T], BF16, kind="Internal").ap()
    cout_c = nc.dram_tensor("cout_c", [NCORE * 512, T], BF16, kind="Internal", addr_space="Shared").ap()

    dbg_out = {}
    dbg_shapes = {"d_xn": [128, NDC * TPC], "d_q": [128, NQH * T],
                  "d_k": [128, T], "d_v": [128, T], "d_ctx": [512, T],
                  "d_e01": [64, TPC], "d_v01": [64, D], "d_w01": [128, 1280]}
    for name in dbg:
        dbg_out[name] = nc.dram_tensor(name, dbg_shapes[name], F32,
                                       kind="ExternalOutput").ap()

    base0 = (nc.sbuf_base + 31) // 32 * 32
    top = nc.sbuf_top
    off = [base0]

    def place(name, shape, dtype, at=None):
        nbytes = int(np.prod(shape[1:])) * dtype.itemsize if hasattr(dtype, "itemsize") else None
        size = shape[1] * {F32: 4, BF16: 2, U8: 1, I32: 4}[dtype]
        pos = off[0] if at is None else at
        t = nc.alloc_sbuf_tensor_at(name, shape, dtype, offset=pos)
        if at is None:
            off[0] = (pos + size + 31) // 32 * 32
        return t

    xn_fm = place("xn_fm", [128, NDC * TPC], BF16)
    regB = place("regB", [128, 32768], BF16)
    k_res = place("k_res", [128, T], BF16)
    v_res = place("v_res", [128, T], BF16)
    v01b = place("v01b", [128, D], BF16)
    e01b = place("e01b", [128, TPC], BF16)
    tri_sb = place("tri_sb", [128, 128], BF16)
    ones_sb = place("ones_sb", [128, 128], BF16)
    id_sb = place("id_sb", [128, 128], BF16)
    cofs_sb = place("cofs_sb", [1, 8], I32)
    R = off[0]
    assert top - R >= 36000, f"R too small: {top - R}"

    regB_at = regB.manual_sbuf_range[0]
    xnt = place("xnt", [128, D], BF16, at=regB_at)
    xbuf = place("xbuf", [128, D], F32, at=regB_at + 8192)
    bnbuf = place("bnbuf", [128, 48], F32, at=regB_at + 24576)
    stats = place("stats", [128, 8], F32, at=regB_at + 24800)
    mtb = [place(f"mtb{i}", [128, 512], BF16, at=R + 1024 * i) for i in range(4)]
    cf_sb = place("cf_sb", [128, 1280], F32, at=R + 4096)
    mk_sb = place("mk_sb", [128, 1280], F32, at=R + 9248)
    enc_sb = place("enc_sb", [128, NDC * 40], BF16, at=R + 24864)
    w01_f = place("w01_f", [128, NDC * 40], F32, at=R + 27424)
    w01_b = place("w01_b", [128, NDC * 40], BF16, at=R + 32576)
    xnw = [place(f"xnw{i}", [128, 1024], BF16, at=R + 2048 * i) for i in range(3)]
    wqw = [place(f"wqw{i}", [128, 1024], BF16, at=R + 6144 + 2048 * i) for i in range(3)]
    wkw = [place(f"wkw{i}", [128, 512], BF16, at=R + 12288 + 1024 * i) for i in range(3)]
    scrq = place("scrq", [128, 512], F32, at=R + 15360)
    scrq2 = place("scrq2", [128, 512], F32, at=R + 17408)
    scrb = [place(f"scrb{i}", [128, 512], BF16, at=R + 19456 + 1024 * i) for i in range(2)]
    kscr = [place(f"kscr{i}", [128, 128], BF16, at=R + 21504 + 256 * i) for i in range(2)]
    ropc = [place(f"ropc{i}", [128, 512], F32, at=R + 22016 + 2048 * i) for i in range(2)]
    rops = [place(f"rops{i}", [128, 512], F32, at=R + 26112 + 2048 * i) for i in range(2)]
    pt_sb = [place(f"pt_sb{i}", [128, 512], BF16, at=R + 1024 * i) for i in range(4)]
    rsr_sb = [place(f"rsr{i}", [128, 512], F32, at=R + 4096 + 2048 * i) for i in range(2)]
    ctxb = [place(f"ctxb{i}", [128, 512], BF16, at=R + 8192 + 1024 * i) for i in range(2)]
    w6 = [place(f"w6_{i}", [128, 512], BF16, at=R + 12288 + 1024 * i) for i in range(8)]
    xld = [place(f"xld{i}", [128, 512], F32, at=R + 4096 + 2048 * i) for i in range(2)]
    outsb = [place(f"outsb{i}", [128, 512], F32, at=R + 8192 + 2048 * i) for i in range(2)]

    ps = [nc.alloc_psum_tensor(f"ps{i}", [128, 512], F32) for i in range(6)]
    psb_guard = [nc.psum_tensor(f"psb{i}", [128, 640], BF16) for i in range(2)]
    psb = [g.__enter__() for g in psb_guard]

    plan = Plan(nc)
    E, S, n = plan.emit, plan.sem, plan.n
    for s in (["s_ld", "xb", "cfl", "enl", "dbgl", "pe", "pe_s", "pe_pv",
               "ac", "scp", "dv", "dvm", "cc", "stc", "sto", "co_ld",
               "bb_ld", "cx_ld", "cst0", "cst1", "so0", "so1", "xl0", "xl1"]
              + [f"mtb{i}" for i in range(4)] + [f"xnw{i}" for i in range(3)]
              + [f"wv{i}" for i in range(3)] + [f"rop{i}" for i in range(2)]
              + [f"w6s{i}" for i in range(8)]):
        S(s)

    E("sync", lambda e: e.dma_start(tri_sb[:], tri), inc="s_ld", inc_amt=16)
    E("sync", lambda e: e.dma_start(ones_sb[:], onesc), inc="s_ld", inc_amt=16)
    E("sync", lambda e: e.dma_start(id_sb[:], identb), inc="s_ld", inc_amt=16)
    E("sync", lambda e: e.dma_start(cofs_sb[:, 0:2], cofs), inc="co_ld", inc_amt=16)
    E("gpsimd", lambda e: e.dma_start(v01b[40:41, :], bbt), inc="bb_ld", inc_amt=16)

    p3_dv_acc = 0
    for ei in range(3):
        w = [] if ei == 0 else [("dv", p3_dv_acc)]
        E("gpsimd", lambda e, ei=ei: e.dma_start(cf_sb[:, 0:1024], c0r[ei]),
          waits=w, inc="cfl", inc_amt=16)
        E("gpsimd", lambda e, ei=ei: e.dma_start(cf_sb[:, 1024:1280], c1r[ei]),
          inc="cfl", inc_amt=16)
        E("gpsimd", lambda e, ei=ei: e.dma_start(mk_sb[:, 0:1024], m0r[ei]),
          inc="cfl", inc_amt=16)
        E("gpsimd", lambda e, ei=ei: e.dma_start(mk_sb[:, 1024:1280], m1r[ei]),
          inc="cfl", inc_amt=16)
        gl = n("cfl")
        if ei == 0:
            E("vector", lambda e: e.tensor_tensor(w01_f[:, 0:1280], cf_sb[:],
                                                  mk_sb[:], ALU.mult),
              waits=[("cfl", gl)], inc="dv")
        else:
            E("vector", lambda e: e.tensor_tensor(mk_sb[:], cf_sb[:], mk_sb[:],
                                                  ALU.mult),
              waits=[("cfl", gl)], inc="dv")
            E("vector", lambda e: e.tensor_tensor(w01_f[:, 0:1280],
                                                  w01_f[:, 0:1280], mk_sb[:],
                                                  ALU.add),
              waits=[("dv", n("dv"))], inc="dv")
        p3_dv_acc = n("dv")
    w01bv = w01_b.ap().rearrange("p (ch r) -> p ch r", r=40)
    E("vector", lambda e: e.tensor_copy(
        w01bv[:, :, 0:32],
        w01_f[:, 0:1024].rearrange("p (ch r) -> p ch r", r=32)),
      waits=[("dv", p3_dv_acc)], inc="dv")
    E("vector", lambda e: e.tensor_copy(
        w01bv[:, :, 32:40],
        w01_f[:, 1024:1280].rearrange("p (ch r) -> p ch r", r=8)),
      inc="dv")
    w01_ready = n("dv")
    E("gpsimd", lambda e: e.dma_start(enc_sb[:], enc01T.rearrange(
        "(ch p) r -> p ch r", p=128)), inc="enl", inc_amt=16)
    E("vector", lambda e: e.memset(e01b[32:64, :], 1.0), inc="dv")
    e01_ones_dv = n("dv")

    mt_r = mt.rearrange("(ch p) c -> ch p c", p=128)
    mt_grp = []
    v01_scp = []
    mload = 0
    mtb_cnt = [0] * 4
    for jb in range(8):
        for ch in range(NDC):
            ring = mload % 4
            w = []
            if mload >= 4:
                w.append(("pe", mt_grp[mload - 4]))
            E("sync", lambda e, ch=ch, jb=jb, ring=ring: e.dma_start(
                mtb[ring][:], mt_r[ch][:, jb * 512:(jb + 1) * 512]),
              waits=w, inc=f"mtb{ring}", inc_amt=16)
            mload += 1
            mtb_cnt[ring] += 1
            w = [(f"mtb{ring}", mtb_cnt[ring] * 16)]
            if ch == 0:
                w.append(("dv", w01_ready))
                if jb >= 2:
                    w.append(("scp", v01_scp[jb - 2]))
            E("tensor", lambda e, ch=ch, jb=jb, ring=ring: e.matmul(
                ps[4 + jb % 2][0:40, :], w01_b[:, ch * 40:(ch + 1) * 40],
                mtb[ring][:], start=(ch == 0), stop=(ch == NDC - 1)),
              waits=w, inc="pe")
            mt_grp.append(n("pe"))
        E("scalar", lambda e, jb=jb: e.mul(
            v01b[0:40, jb * 512:(jb + 1) * 512], ps[4 + jb % 2][0:40, :], 0.5),
          waits=[("pe", n("pe"))], inc="scp")
        v01_scp.append(n("scp"))


    xc_t = xc.rearrange("(t p) c -> t p c", p=128)
    xn_v = xn_fm.ap().rearrange("p (d t) -> p d t", t=TPC)
    p1_ts_dv = []
    p1_tr_pe = []
    p1_evac_scp = []
    for tt in range(NTT):
        w = [("dv", p1_ts_dv[tt - 1])] if tt >= 1 else []
        E("sync", lambda e, tt=tt: e.dma_start(xbuf[:], xc_t[tt]),
          waits=w, inc="xb", inc_amt=16)
        sld = (tt + 1) * 16
        for k in range(8):
            wk = [("xb", sld)] if k == 0 else []
            E("vector", lambda e, k=k: e.bn_stats(
                bnbuf[:, 6 * k:6 * k + 6], xbuf[:, 512 * k:512 * k + 512]),
              waits=wk, inc="dv" if k == 7 else None)
        E("vector", lambda e: e.bn_aggr(stats[:, 0:2], bnbuf[:]),
          waits=[("dv", n("dv"))], inc="dv")
        E("vector", lambda e: e.tensor_single_scalar(
            stats[:, 4:5], stats[:, 1:2], 1e-5, ALU.add),
          waits=[("dv", n("dv"))], inc="dv")
        E("scalar", lambda e: e.activation(stats[:, 2:3], stats[:, 4:5],
                                           AF.Ln),
          waits=[("dv", n("dv"))], inc="scp")
        E("scalar", lambda e: e.activation(stats[:, 3:4], stats[:, 2:3],
                                           AF.Exp, scale=-0.5),
          waits=[("scp", n("scp"))], inc="scp")
        w = [("scp", n("scp"))]
        if tt >= 1:
            w.append(("pe", p1_tr_pe[tt - 1]))
        E("vector", lambda e: e.tensor_scalar(
            xnt[:], xbuf[:], stats[:, 0:1], stats[:, 3:4],
            ALU.subtract, ALU.mult), waits=w, inc="dv")
        p1_ts_dv.append(n("dv"))
        for g4 in range(8):
            pb = (tt * 8 + g4) % 2
            for j in range(4):
                dchunk = g4 * 4 + j
                w = []
                if j == 0:
                    w.append(("dv", p1_ts_dv[tt]))
                    if tt == 0 and g4 == 0:
                        w.append(("s_ld", 3 * 16))
                    if len(p1_evac_scp) >= 2:
                        w.append(("scp", p1_evac_scp[-2]))
                E("tensor", lambda e, pb=pb, j=j, dchunk=dchunk: e.transpose(
                    psb[pb][:, 128 * j:128 * j + 128],
                    xnt[:, 128 * dchunk:128 * dchunk + 128], id_sb[:]),
                  waits=w, inc="pe" if j == 3 else None)
            E("scalar", lambda e, pb=pb, g4=g4, tt=tt: e.copy(
                xn_v[:, g4 * 4:g4 * 4 + 4, tt * 128:tt * 128 + 128],
                psb[pb].ap()[:, 0:512].rearrange("p (j c) -> p j c", j=4)),
              waits=[("pe", n("pe"))], inc="scp")
            p1_evac_scp.append(n("scp"))
        p1_tr_pe.append(n("pe"))
    xn_done_scp = n("scp")

    cin_xn_t = cin_xn.rearrange("(d p) c -> d p c", p=128)
    for dchunk in range(NDC):
        E("scalar", lambda e, dchunk=dchunk: e.dma_start(
            cin_xn_t[dchunk], xn_fm[:, dchunk * TPC:(dchunk + 1) * TPC]),
          waits=[("scp", xn_done_scp)] if dchunk == 0 else [],
          inc="stc", inc_amt=16)
    E("gpsimd", lambda e: e.collective_compute(
        "AllGather", ALU.bypass, replica_groups=[list(range(NCORE))],
        ins=[cin_xn.opt()], outs=[cout_xn.opt()]),
      waits=[("stc", n("stc"))], inc="cc")

    if "d_xn" in dbg_out:
        for i in range(NDC * TPC // 512):
            E("gpsimd", lambda e, i=i: e.dma_start(
                dbg_out["d_xn"][:, i * 512:(i + 1) * 512],
                xn_fm[:, i * 512:(i + 1) * 512]),
              waits=[("scp", xn_done_scp)], inc="dbgl", inc_amt=16)

    if upto == "p1":
        with nc.Block() as blk:
            plan.build(blk)
        return nc

    for tc in range(2):
        for ch in range(NDC):
            w = []
            if ch == 0:
                w = [("enl", 16), ("scp", v01_scp[tc + 6]),
                     ("scp", xn_done_scp)]
            E("tensor", lambda e, ch=ch, tc=tc: e.matmul(
                ps[4 + tc][0:40, :], enc_sb[:, ch * 40:(ch + 1) * 40],
                xn_fm[:, ch * TPC + tc * 512: ch * TPC + tc * 512 + 512],
                start=(ch == 0), stop=(ch == NDC - 1)),
              waits=w, inc="pe" if ch == NDC - 1 else None)
        E("scalar", lambda e, tc=tc: e.copy(
            e01b[0:40, tc * 512:(tc + 1) * 512], ps[4 + tc][0:40, :]),
          waits=[("pe", n("pe")), ("dv", e01_ones_dv)], inc="scp")
    p3b_done_scp = n("scp")
    p3b_done_pe = n("pe")

    if "d_w01" in dbg_out:
        E("gpsimd", lambda e: e.dma_start(dbg_out["d_w01"], w01_b[:]),
          waits=[("dv", w01_ready)], inc="dbgl", inc_amt=16)
    if "d_v01" in dbg_out:
        for i in range(8):
            E("gpsimd", lambda e, i=i: e.dma_start(
                dbg_out["d_v01"][0:64, i * 512:(i + 1) * 512],
                v01b[0:64, i * 512:(i + 1) * 512]),
              waits=[("scp", p3b_done_scp)], inc="dbgl", inc_amt=16)
    if "d_e01" in dbg_out:
        for i in range(2):
            E("gpsimd", lambda e, i=i: e.dma_start(
                dbg_out["d_e01"][0:64, i * 512:(i + 1) * 512],
                e01b[0:64, i * 512:(i + 1) * 512]),
              waits=[("scp", p3b_done_scp)], inc="dbgl", inc_amt=16)

    if upto == "p3b":
        with nc.Block() as blk:
            plan.build(blk)
        return nc

    cout_r = cout_xn.rearrange("(r d p) c -> r d p c", p=128, d=NDC)
    wq_r = wq_my.rearrange("(d p) c -> d p c", p=128)
    wkv_r = wkv_my.rearrange("(d p) c -> d p c", p=128)
    q_fm = regB.ap().rearrange("p (h t) -> p h t", h=NQH)

    grp_pe = []
    rop_cnt = [0, 0]
    xnw_cnt = [0] * 3
    wv_cnt = [0] * 3
    ld_i = 0
    q_dv = {}
    kv_dv = {}
    v_scp = {}
    evac_scp = []
    qtr_pe = {}
    ktr_pe = {}
    tab_tch_dv = []

    for tch in range(NTCH):
        r, sl = tch // 2, tch % 2
        tpar = tch % 2
        w = []
        if tch < 2:
            w.append(("pe", p3b_done_pe))
        else:
            w.append(("dv", tab_tch_dv[tch - 2]))
        E("sync", lambda e, tch=tch, tpar=tpar: e.dma_start(
            ropc[tpar][:], cosT[:, tch * 512:(tch + 1) * 512]),
          waits=w, inc=f"rop{tpar}", inc_amt=16)
        E("sync", lambda e, tch=tch, tpar=tpar: e.dma_start(
            rops[tpar][:], sinT[:, tch * 512:(tch + 1) * 512]),
          inc=f"rop{tpar}", inc_amt=16)
        rop_cnt[tpar] += 2
        tab_par, tab_here = tpar, rop_cnt[tpar] * 16

        for dp in range(16):
            ring = ld_i % 3
            w = []
            if ld_i == 0:
                w.append(("cc", 1))
            if ld_i < 3:
                w.append(("pe", p3b_done_pe))
            if ld_i >= 3:
                w.append(("pe", grp_pe[ld_i - 3]))
            E("sync", lambda e, r=r, sl=sl, dp=dp, ring=ring: e.dma_start(
                xnw[ring].ap().rearrange("p (d c) -> p d c", d=2),
                cout_r[r][dp * 2:dp * 2 + 2, :, sl * 512:(sl + 1) * 512]
                .rearrange("d p c -> p d c")),
              waits=w, inc=f"xnw{ring}", inc_amt=16)
            wg = [("pe", grp_pe[ld_i - 3] if ld_i >= 3 else p3b_done_pe)]
            E("gpsimd", lambda e, dp=dp, ring=ring: e.dma_start(
                wqw[ring].ap().rearrange("p (d c) -> p d c", d=2),
                wq_r[dp * 2:dp * 2 + 2, :, :].rearrange("d p c -> p d c")),
              waits=wg, inc=f"wv{ring}", inc_amt=16)
            E("gpsimd", lambda e, dp=dp, ring=ring: e.dma_start(
                wkw[ring].ap().rearrange("p (d c) -> p d c", d=2),
                wkv_r[dp * 2:dp * 2 + 2, :, :].rearrange("d p c -> p d c")),
              inc=f"wv{ring}", inc_amt=16)
            ld_i += 1
            xnw_cnt[ring] += 1
            wv_cnt[ring] += 2
            xw_here, w2_here = xnw_cnt[ring] * 16, wv_cnt[ring] * 16

            for dd in range(2):
                for tt in range(4):
                    gtile = tch * 4 + tt
                    w = []
                    if dd == 0 and tt == 0:
                        w = [(f"xnw{ring}", xw_here), (f"wv{ring}", w2_here)]
                    if dp == 0 and dd == 0 and (gtile - 4) in q_dv:
                        w.append(("dv", q_dv[gtile - 4]))
                    E("tensor", lambda e, dd=dd, tt=tt, ring=ring, dp=dp: e.matmul(
                        ps[tt][:], xnw[ring][:, dd * 512 + tt * 128: dd * 512 + tt * 128 + 128],
                        wqw[ring][:, dd * 512:(dd + 1) * 512],
                        start=(dp == 0 and dd == 0), stop=(dp == 15 and dd == 1),
                        skip_group_check=True), waits=w)
                for tt in range(4):
                    gtile = tch * 4 + tt
                    kvbk, kvc = 4 + tt // 2, (tt % 2) * 256
                    w = []
                    if dp == 0 and dd == 0 and tt % 2 == 0:
                        if tch == 0:
                            w.append(("scp", p3b_done_scp))
                        for gprev in (gtile - 4, gtile - 3):
                            if gprev in kv_dv:
                                w.append(("dv", kv_dv[gprev]))
                                w.append(("scp", v_scp[gprev]))
                    E("tensor", lambda e, dd=dd, tt=tt, ring=ring, kvbk=kvbk, kvc=kvc, dp=dp: e.matmul(
                        ps[kvbk][:, kvc:kvc + 256],
                        xnw[ring][:, dd * 512 + tt * 128: dd * 512 + tt * 128 + 128],
                        wkw[ring][:, dd * 256:(dd + 1) * 256],
                        start=(dp == 0 and dd == 0 and tt % 2 == 0),
                        stop=(dp == 15 and dd == 1),
                        skip_group_check=True),
                      waits=w, inc="pe" if (dd == 1 and tt == 3) else None)
            grp_pe.append(n("pe"))
        tch_mm_pe = n("pe")

        for tt in range(4):
            gtile = tch * 4 + tt
            spar = gtile % 2
            kvbk, kvc = 4 + tt // 2, (tt % 2) * 256
            tcol = tt * 128
            cosb = ropc[tpar].ap().rearrange(
                "p (a f) -> p a f", a=4)[:, tt:tt + 1, :].broadcast_to((128, 4, 128))
            sinb = rops[tpar].ap().rearrange(
                "p (a f) -> p a f", a=4)[:, tt:tt + 1, :].broadcast_to((128, 4, 128))
            cos2 = ropc[tpar].ap().rearrange(
                "p (a h f) -> p a h f", a=4, h=2)[:, tt:tt + 1, :, :].broadcast_to((128, 4, 2, 64))
            sin2 = rops[tpar].ap().rearrange(
                "p (a h f) -> p a h f", a=4, h=2)[:, tt:tt + 1, :, :].broadcast_to((128, 4, 2, 64))
            qv = ps[tt].ap().rearrange("p (b h f) -> p b h f", b=4, h=2)
            s2v = scrq2.ap().rearrange("p (b h f) -> p b h f", b=4, h=2)
            w = [("pe", tch_mm_pe), (f"rop{tab_par}", tab_here)]
            E("vector", lambda e, qv=qv, s2v=s2v, sin2=sin2: e.tensor_tensor(
                s2v[:, :, 0, :], qv[:, :, 1, :], sin2[:, :, 0, :], ALU.mult),
              waits=w, inc="dv")
            E("vector", lambda e, qv=qv, s2v=s2v, sin2=sin2: e.tensor_tensor(
                s2v[:, :, 1, :], qv[:, :, 0, :], sin2[:, :, 1, :], ALU.mult),
              inc="dv")
            E("vector", lambda e, tt=tt, cosb=cosb: e.tensor_tensor(
                scrq.ap().rearrange("p (b f) -> p b f", b=4),
                ps[tt].ap().rearrange("p (b f) -> p b f", b=4), cosb, ALU.mult),
              inc="dv")
            q_dv[gtile] = n("dv")
            w = [("dv", n("dv"))]
            if spar in qtr_pe:
                w.append(("pe", qtr_pe[spar]))
            E("vector", lambda e, spar=spar: e.tensor_tensor(
                scrb[spar][:], scrq[:], scrq2[:], ALU.add), waits=w, inc="dv")
            scrb_dv = n("dv")
            kv2 = ps[kvbk].ap()[:, kvc:kvc + 128].rearrange(
                "p (h f) -> p h f", h=2)
            s2k = scrq2.ap()[:, 0:128].rearrange("p (h f) -> p h f", h=2)
            c1b = ropc[tpar].ap().rearrange(
                "p (a h f) -> p a h f", a=4, h=2)[:, tt, :, :]
            s1b = rops[tpar].ap().rearrange(
                "p (a h f) -> p a h f", a=4, h=2)[:, tt, :, :]
            E("vector", lambda e, kv2=kv2, s2k=s2k, s1b=s1b: e.tensor_tensor(
                s2k[:, 0, :], kv2[:, 1, :], s1b[:, 0, :], ALU.mult),
              waits=[("dv", scrb_dv)], inc="dv")
            E("vector", lambda e, kv2=kv2, s2k=s2k, s1b=s1b: e.tensor_tensor(
                s2k[:, 1, :], kv2[:, 0, :], s1b[:, 1, :], ALU.mult), inc="dv")
            E("vector", lambda e, kvbk=kvbk, kvc=kvc, c1b=c1b: e.tensor_tensor(
                scrq.ap()[:, 0:128].rearrange("p (h f) -> p h f", h=2),
                ps[kvbk].ap()[:, kvc:kvc + 128].rearrange("p (h f) -> p h f", h=2),
                c1b, ALU.mult), inc="dv")
            kv_dv[gtile] = n("dv")
            if tt == 3:
                tab_tch_dv.append(n("dv"))
            w = [("dv", n("dv"))]
            if spar in ktr_pe:
                w.append(("pe", ktr_pe[spar]))
            E("vector", lambda e, spar=spar: e.tensor_tensor(
                kscr[spar][:], scrq[:, 0:128], scrq2[:, 0:128], ALU.add),
              waits=w, inc="dv")
            kscr_dv = n("dv")
            E("scalar", lambda e, kvbk=kvbk, kvc=kvc, gtile=gtile: e.copy(
                v_res[:, gtile * 128:(gtile + 1) * 128],
                ps[kvbk][:, kvc + 128:kvc + 256]),
              waits=[("pe", tch_mm_pe)], inc="scp")
            v_scp[gtile] = n("scp")
            for h in range(NQH):
                w = []
                if h == 0:
                    w = [("dv", scrb_dv)]
                    if len(evac_scp) >= 2:
                        w.append(("scp", evac_scp[-2]))
                E("tensor", lambda e, h=h, spar=spar: e.transpose(
                    psb[spar][:, h * 128:(h + 1) * 128],
                    scrb[spar][:, h * 128:(h + 1) * 128], id_sb[:]),
                  waits=w, inc="pe" if h == 3 else None)
            qtr_pe[spar] = n("pe")
            E("tensor", lambda e, spar=spar: e.transpose(
                psb[spar][:, 512:640], kscr[spar][:], id_sb[:]),
              waits=[("dv", kscr_dv)], inc="pe")
            ktr_pe[spar] = n("pe")
            E("scalar", lambda e, spar=spar, gtile=gtile: e.copy(
                q_fm[:, :, gtile * 128:(gtile + 1) * 128],
                psb[spar].ap()[:, 0:512].rearrange("p (h c) -> p h c", h=4)),
              waits=[("pe", n("pe"))], inc="scp")
            E("scalar", lambda e, spar=spar, gtile=gtile: e.copy(
                k_res[:, gtile * 128:(gtile + 1) * 128], psb[spar][:, 512:640]),
              inc="scp")
            evac_scp.append(n("scp"))
    p2_done_scp = n("scp")

    for nm, buf, wd in (("d_q", regB, NQH * T), ("d_k", k_res, T), ("d_v", v_res, T)):
        if nm in dbg_out:
            for i in range(wd // 512):
                E("gpsimd", lambda e, i=i, nm=nm, buf=buf: e.dma_start(
                    dbg_out[nm][:, i * 512:(i + 1) * 512],
                    buf[:, i * 512:(i + 1) * 512]),
                  waits=[("scp", p2_done_scp)], inc="dbgl", inc_amt=16)

    if upto == "p2":
        with nc.Block() as blk:
            plan.build(blk)
        return nc

    for g in reversed(psb_guard):
        g.__exit__(None, None, None)
    ps6 = [nc.alloc_psum_tensor(f"ps6_{i}", [128, 512], F32) for i in range(2)]
    p6b = ps + ps6
    units = []
    gid = 0
    for hq in range(NQH):
        for qch in range(NTCH):
            nkt = (qch + 1) * 4
            for kt in range(nkt):
                diag = kt >= qch * 4
                qoff = (kt - qch * 4) * 128 if diag else 0
                units.append((hq, qch, kt, diag, qoff, 512 - qoff,
                              hq * T + qch * 512 + qoff, gid, kt == nkt - 1))
            gid += 1
    NU = len(units)
    ctx_of = lambda g: (ps6[0] if g % 2 else ps[4])
    rs_of = lambda g: (ps6[1] if g % 2 else ps[5])

    mask_cnt = [0]
    mask_idx = {}
    ev_dv = []
    grp_pv = {}
    cst_cnt = [0, 0]

    def emit_S(u):
        hq, qch, kt, diag, qoff, N, qcol, g, last = units[u]
        w = []
        if u == 0:
            w.append(("scp", p2_done_scp))
        if u >= 4:
            w.append(("ac", u - 3))
        E("tensor", lambda e, kt=kt, qcol=qcol, qoff=qoff, N=N, sbk=u % 4: e.matmul(
            ps[sbk][:, qoff:qoff + N], k_res[:, kt * 128:(kt + 1) * 128],
            regB[:, qcol:qcol + N], start=True, stop=True),
          waits=w, inc="pe_s")

    def emit_exp(u):
        hq, qch, kt, diag, qoff, N, qcol, g, last = units[u]
        w = [("pe_s", u + 1)]
        if u >= 4:
            w.append(("pe_pv", u - 3))
        E("scalar", lambda e, sbk=u % 4, qoff=qoff, N=N, u=u: e.activation(
            pt_sb[u % 4][:, qoff:qoff + N], ps[sbk][:, qoff:qoff + N],
            AF.Exp, scale=SCALE),
          waits=w, inc="ac")
        if diag:
            E("vector", lambda e, u=u, qoff=qoff: e.tensor_tensor(
                pt_sb[u % 4][:, qoff:qoff + 128],
                pt_sb[u % 4][:, qoff:qoff + 128], tri_sb[:], ALU.mult),
              waits=[("ac", u + 1)], inc="dvm")
            mask_cnt[0] += 1
            mask_idx[u] = mask_cnt[0]

    def emit_rspv(u):
        hq, qch, kt, diag, qoff, N, qcol, g, last = units[u]
        w = [("ac", u + 1)]
        if diag:
            w.append(("dvm", mask_idx[u]))
        if kt == 0 and g >= 2:
            w.append(("dv", ev_dv[g - 2]))
        E("tensor", lambda e, u=u, qoff=qoff, N=N, kt=kt, g=g, last=last: e.matmul(
            rs_of(g)[:, qoff:qoff + N], ones_sb[:],
            pt_sb[u % 4][:, qoff:qoff + N],
            start=(kt == 0), stop=last, skip_group_check=True), waits=w)
        E("tensor", lambda e, u=u, qoff=qoff, N=N, kt=kt, g=g, last=last: e.matmul(
            ctx_of(g)[:, qoff:qoff + N], v_res[:, kt * 128:(kt + 1) * 128],
            pt_sb[u % 4][:, qoff:qoff + N],
            start=(kt == 0), stop=last, skip_group_check=True), inc="pe_pv")
        if last:
            grp_pv[g] = n("pe_pv")
            emit_evac(g, hq, qch)

    def emit_evac(g, hq, qch):
        epar = g % 2
        w = [("pe_pv", grp_pv[g])]
        if g >= 2:
            w.append(("dv", ev_dv[g - 2]))
        E("vector", lambda e, epar=epar, g=g: e.reciprocal(
            rsr_sb[epar][:], rs_of(g)[:]), waits=w, inc="dv")
        w = [("dv", n("dv"))]
        if g >= 2:
            w.append((f"cst{epar}", cst_cnt[epar] * 16))
        E("vector", lambda e, epar=epar, g=g: e.tensor_tensor(
            ctxb[epar][:], ctx_of(g)[:], rsr_sb[epar][:], ALU.mult),
          waits=w, inc="dv")
        ev_dv.append(n("dv"))
        E("scalar", lambda e, epar=epar, hq=hq, qch=qch: e.dma_start(
            cin_c[hq * 128:(hq + 1) * 128, qch * 512:(qch + 1) * 512],
            ctxb[epar][:]),
          waits=[("dv", n("dv"))], inc=f"cst{epar}", inc_amt=16)
        cst_cnt[epar] += 1

    for u in range(NU):
        emit_S(u)
        emit_exp(u)
        if u >= 2:
            emit_rspv(u - 2)
    emit_rspv(NU - 2)
    emit_rspv(NU - 1)
    p4_ac = n("ac")
    p4_pv = n("pe_pv")
    p4_dv_last = ev_dv[-1]

    wb_r = wbT.rearrange("(d p) c -> d p c", p=128)
    w6_cnt = [0] * 8
    for d in range(8):
        E("gpsimd", lambda e, d=d: e.dma_start(
            w6[d][:], wb_r[d][:, 0:512]),
          waits=[("scp", p2_done_scp)] if d == 0 else [],
          inc=f"w6s{d}", inc_amt=16)
        w6_cnt[d] += 1

    E("gpsimd", lambda e: e.collective_compute(
        "AllGather", ALU.bypass, replica_groups=[list(range(NCORE))],
        ins=[cin_c.opt()], outs=[cout_c.opt()]),
      waits=[("cst0", cst_cnt[0] * 16), ("cst1", cst_cnt[1] * 16)], inc="cc")

    if "d_ctx" in dbg_out:
        for i in range(4):
            E("gpsimd", lambda e, i=i: e.dma_start(
                dbg_out["d_ctx"][i * 128:(i + 1) * 128, :],
                cin_c[i * 128:(i + 1) * 128, :]),
              waits=[("cc", 2)], inc="dbgl", inc_amt=16)

    if upto == "p4":
        with nc.Block() as blk:
            plan.build(blk)
        return nc


    cout_c_p = cout_c.rearrange("(d p) c -> p d c", p=128)

    def load_ctx_branched(e):
        last = {}
        with e.register("cid") as cid:
            e.reg_load(cid, cofs_sb[0:1, 1:2])
            for k in range(NCORE):
                with e.If_eq(cid, k):
                    e.dma_start(
                        regB.ap().rearrange("p (d t) -> p d t", t=TPC),
                        cout_c_p[:, :, k * TPC:(k + 1) * TPC]
                    ).then_inc(plan.handles["cx_ld"], 16)
            e.end_ifs()
        return None

    S("cx_ld")
    plan.counts["cx_ld"] += 16
    plan.items["sync"].append((load_ctx_branched,
                               [(S("cc"), 2), (S("co_ld"), 16)], None, 1))
    ctx_ld = n("cx_ld")

    wo_r = wo.rearrange("(d p) c -> d p c", p=128)
    outv = out.rearrange("(t p) c -> t p c", p=128)

    wgrp_pe = []
    w6i = 0
    p6_ev_dv = []
    nst = 0
    xl_cnt = [0, 0]
    so_cnt = [0, 0]
    for ob in range(8):
        for si, (wsrc, lhs) in enumerate(((wb_r, xn_fm), (wo_r, regB))):
            for d in range(NDC):
                ring = w6i % 8
                if w6i >= 8:
                    w = []
                    if w6i >= 8:
                        w.append(("pe", wgrp_pe[w6i - 8]))
                    E("gpsimd", lambda e, d=d, ob=ob, ring=ring, wsrc=wsrc: e.dma_start(
                        w6[ring][:], wsrc[d][:, ob * 512:(ob + 1) * 512]),
                      waits=w, inc=f"w6s{ring}", inc_amt=16)
                    w6_cnt[ring] += 1
                w6i += 1
                wld = w6_cnt[ring] * 16
                for tt in range(NTT):
                    w = []
                    if tt == 0:
                        w = [(f"w6s{ring}", wld)]
                    if si == 0 and d == 0:
                        if ob == 0:
                            if tt == 0:
                                w.append(("ac", p4_ac))
                                w.append(("scp", p3b_done_scp))
                                w.append(("bb_ld", 16))
                            if tt == 4:
                                w.append(("dv", p4_dv_last))
                        else:
                            w.append(("dv", p6_ev_dv[(ob - 1) * 8 + tt]))
                    if si == 1 and d == 0 and tt == 0 and ob == 0:
                        w.append(("cx_ld", ctx_ld))
                    first = (si == 0 and d == 0)
                    E("tensor", lambda e, d=d, tt=tt, ring=ring, lhs=lhs, first=first: e.matmul(
                        p6b[tt][:],
                        lhs[:, d * TPC + tt * 128: d * TPC + tt * 128 + 128],
                        w6[ring][:], start=first, stop=False,
                        skip_group_check=True),
                      waits=w, inc="pe" if tt == NTT - 1 else None)
                wgrp_pe.append(n("pe"))
        for tt in range(NTT):
            E("tensor", lambda e, tt=tt, ob=ob: e.matmul(
                p6b[tt][:], e01b[0:41, tt * 128:(tt + 1) * 128],
                v01b[0:41, ob * 512:(ob + 1) * 512],
                start=False, stop=True, skip_group_check=True),
              inc="pe_pv")
            pv_here = n("pe_pv")
            xpar = (ob * NTT + tt) % 2
            w = []
            if len(p6_ev_dv) >= 2:
                w.append(("dv", p6_ev_dv[-2]))
            elif ob == 0 and tt < 2:
                w.append(("dv", p4_dv_last))
            E("sync", lambda e, ob=ob, tt=tt, xpar=xpar: e.dma_start(
                xld[xpar][:], xc_t[tt][:, ob * 512:(ob + 1) * 512]),
              waits=w, inc=f"xl{xpar}", inc_amt=16)
            xl_cnt[xpar] += 1
            w = [("pe_pv", pv_here), (f"xl{xpar}", xl_cnt[xpar] * 16)]
            if nst >= 2:
                w.append((f"so{nst % 2}", so_cnt[nst % 2] * 16))
            elif nst < 2:
                w.append((f"cst{nst % 2}", cst_cnt[nst % 2] * 16))
            E("vector", lambda e, tt=tt, xpar=xpar, opar=nst % 2: e.tensor_tensor(
                outsb[opar][:], p6b[tt][:], xld[xpar][:], ALU.add),
              waits=w, inc="dv")
            p6_ev_dv.append(n("dv"))
            E("scalar", lambda e, tt=tt, ob=ob, opar=nst % 2: e.dma_start(
                outv[tt][:, ob * 512:(ob + 1) * 512], outsb[opar][:]),
              waits=[("dv", n("dv"))], inc=f"so{nst % 2}", inc_amt=16)
            so_cnt[nst % 2] += 1
            nst += 1

    E("sync", lambda e: None, waits=[("so0", so_cnt[0] * 16),
                                     ("so1", so_cnt[1] * 16)])
    E("gpsimd", lambda e: None, waits=[("dbgl", n("dbgl"))])

    with nc.Block() as blk:
        plan.build(blk)
    return nc



def kernel(L0_attn_resid, L0_attn_Wq, L0_attn_Wk, L0_attn_Wv, L0_attn_Wo,
           L0_ffn_W_base, L0_ffn_b_base, L0_ffn_enc0, L0_ffn_enc1,
           L0_ffn_coefs0, L0_ffn_coefs1, L0_ffn_mask0, L0_ffn_mask1):
    x = np.ascontiguousarray(np.asarray(L0_attn_resid, dtype=np.float32)[0])
    Wq = np.asarray(L0_attn_Wq, dtype=np.float32)
    Wk = np.asarray(L0_attn_Wk, dtype=np.float32)
    Wv = np.asarray(L0_attn_Wv, dtype=np.float32)
    Wo = np.ascontiguousarray(np.asarray(L0_attn_Wo, dtype=np.float32))
    WbT = np.ascontiguousarray(np.asarray(L0_ffn_W_base, dtype=np.float32).T)
    bbv = np.ascontiguousarray(np.asarray(L0_ffn_b_base, np.float32)[None, :])
    enc01T = np.ascontiguousarray(
        np.concatenate([np.asarray(L0_ffn_enc0, np.float32).T,
                        np.asarray(L0_ffn_enc1, np.float32).T], axis=1))
    c0 = np.ascontiguousarray(
        np.asarray(L0_ffn_coefs0, np.float32).reshape(3, 32, 128, 32)
        .transpose(0, 2, 1, 3).reshape(3, 128, 1024))
    c1 = np.ascontiguousarray(
        np.asarray(L0_ffn_coefs1, np.float32).reshape(3, 32, 128, 8)
        .transpose(0, 2, 1, 3).reshape(3, 128, 256))
    m0 = np.ascontiguousarray(
        np.asarray(L0_ffn_mask0).astype(np.uint8).reshape(3, 32, 128, 32)
        .transpose(0, 2, 1, 3).reshape(3, 128, 1024))
    m1 = np.ascontiguousarray(
        np.asarray(L0_ffn_mask1).astype(np.uint8).reshape(3, 32, 128, 8)
        .transpose(0, 2, 1, 3).reshape(3, 128, 256))
    MT = build_MT().astype(BFNP)

    cosv, sinv = rope_tables()
    cos_t = np.empty((128, GT * 128), np.float32)
    sin_t = np.empty((128, GT * 128), np.float32)
    for g in range(GT):
        cb = cosv[g * 128:(g + 1) * 128]
        sb = sinv[g * 128:(g + 1) * 128]
        cos_t[:, g * 128:g * 128 + 64] = cb
        cos_t[:, g * 128 + 64:g * 128 + 128] = cb
        sin_t[:, g * 128:g * 128 + 64] = -sb
        sin_t[:, g * 128 + 64:g * 128 + 128] = sb

    triv = np.triu(np.ones((128, 128), np.float32)).astype(BFNP)
    onesv = np.ones((128, 128), np.float32).astype(BFNP)
    identv = np.eye(128, dtype=np.float32).astype(BFNP)

    nc = build_program()
    in_maps = []
    for c in range(NCORE):
        in_maps.append({
            "xc": np.ascontiguousarray(x[c * TPC:(c + 1) * TPC]),
            "wq_my": np.ascontiguousarray(Wq[:, c * 512:(c + 1) * 512]),
            "wkv_my": np.ascontiguousarray(
                np.concatenate([Wk[:, c * 128:(c + 1) * 128],
                                Wv[:, c * 128:(c + 1) * 128]], axis=1)),
            "wo": Wo, "wbT": WbT, "mt": MT, "enc01T": enc01T,
            "c0r": c0, "c1r": c1, "m0r": m0, "m1r": m1, "bb": bbv,
            "cosT": cos_t, "sinT": sin_t,
            "tri": triv, "onesc": onesv, "identb": identv,
            "cofs": np.array([[c * TPC, c]], np.int32),
        })
    res = run_bass_kernel_spmd(nc, in_maps, core_ids=list(range(NCORE)))
    pieces = [res.results[c]["out"] for c in range(NCORE)]
    return np.concatenate(pieces, axis=0)[None, :, :]


# version 04
# speedup vs baseline: 1.0212x; 1.0212x over previous
import math
import numpy as np
import ml_dtypes
import concourse.bass as bass
import concourse.mybir as mybir
from concourse.bass_utils import run_bass_kernel_spmd

F32 = mybir.dt.float32
BF16 = mybir.dt.bfloat16
U8 = mybir.dt.uint8
I32 = mybir.dt.int32
AF = mybir.ActivationFunctionType
ALU = mybir.AluOpType

BFNP = ml_dtypes.bfloat16

NCORE = 8
T = 8192
D = 4096
DH = 128
TPC = T // NCORE
NTT = TPC // 128
NDC = D // 128
NQH = 4
NTCH = T // 512
GT = T // 128
SCALE = 1.0 / math.sqrt(DH)

ENGINES = ("sync", "scalar", "vector", "tensor", "gpsimd")


def build_MT(n=D):
    j = np.arange(n, dtype=np.float64)[:, None]
    M = np.zeros((n, n), dtype=np.float64)
    M[:, 0] = 1.0 / math.sqrt(n)
    max_f = (n - 1) // 2
    f = np.arange(1, max_f + 1, dtype=np.float64)[None, :]
    ang = 2.0 * math.pi * f * j / n
    M[:, 1:1 + 2 * max_f:2] = math.sqrt(2.0 / n) * np.cos(ang)
    M[:, 2:2 + 2 * max_f:2] = math.sqrt(2.0 / n) * np.sin(ang)
    M[:, n - 1] = ((-1.0) ** j[:, 0]) / math.sqrt(n)
    return np.ascontiguousarray(M.T)


def rope_tables():
    theta, factor, lo, hi, L0 = 500000.0, 32.0, 1.0, 4.0, 8192
    idx = np.arange(0, DH, 2, dtype=np.float64) / DH
    freq = 1.0 / (2.0 * math.pi) * theta ** (-idx)
    freq_low, freq_high = lo / L0, hi / L0
    freq_scaled = np.where(freq < freq_low, freq / factor, freq)
    smooth = np.clip((L0 * freq - lo) / (hi - lo), 0.0, 1.0)
    freq_smooth = (1.0 - smooth) * (freq / factor) + smooth * freq
    is_mid = (freq >= freq_low) & (freq <= freq_high)
    freq = np.where(is_mid, freq_smooth, freq_scaled)
    pos = np.arange(T, dtype=np.float64)
    phase = 2.0 * math.pi * pos[:, None] * freq[None, :]
    return np.cos(phase), np.sin(phase)


class Plan:

    def __init__(self, nc):
        self.nc = nc
        self.items = {e: [] for e in ENGINES}
        self.handles = {}
        self.counts = {}

    def sem(self, name):
        if name not in self.handles:
            self.handles[name] = self.nc.alloc_semaphore(name)
            self.counts[name] = 0
        return name

    def n(self, name):
        return self.counts[name]

    def emit(self, eng, fn, waits=(), inc=None, inc_amt=1):
        if inc is not None:
            self.sem(inc)
            self.counts[inc] += inc_amt
        ws = [(self.sem(s), int(v)) for (s, v) in waits if v > 0]
        self.items[eng].append((fn, ws, inc, inc_amt))

    def build(self, blk):
        for engname in ("gpsimd", "scalar", "vector", "tensor", "sync"):
            items = self.items[engname]
            if not items:
                continue

            def make_body(items=items):
                def body(e):
                    for fn, ws, inc, amt in items:
                        for s, v in ws:
                            e.wait_ge(self.handles[s], v)
                        instr = fn(e)
                        if inc is not None:
                            instr.then_inc(self.handles[inc], amt)
                return body
            getattr(blk, engname)(make_body())


def build_program(dbg=(), upto="p6"):
    nc = bass.Bass("TRN2", target_bir_lowering=False, debug=False)

    xc = nc.dram_tensor("xc", [TPC, D], F32, kind="ExternalInput").ap()
    wq_my = nc.dram_tensor("wq_my", [D, 512], F32, kind="ExternalInput").ap()
    wkv_my = nc.dram_tensor("wkv_my", [D, 256], F32, kind="ExternalInput").ap()
    wo = nc.dram_tensor("wo", [D, D], F32, kind="ExternalInput").ap()
    wbT = nc.dram_tensor("wbT", [D, D], F32, kind="ExternalInput").ap()
    mt = nc.dram_tensor("mt", [D, D], BF16, kind="ExternalInput").ap()
    enc01T = nc.dram_tensor("enc01T", [D, 40], F32, kind="ExternalInput").ap()
    c0r = nc.dram_tensor("c0r", [3, 128, 1024], F32, kind="ExternalInput").ap()
    c1r = nc.dram_tensor("c1r", [3, 128, 256], F32, kind="ExternalInput").ap()
    m0r = nc.dram_tensor("m0r", [3, 128, 1024], U8, kind="ExternalInput").ap()
    m1r = nc.dram_tensor("m1r", [3, 128, 256], U8, kind="ExternalInput").ap()
    bbt = nc.dram_tensor("bb", [1, D], F32, kind="ExternalInput").ap()
    cosT = nc.dram_tensor("cosT", [128, GT * 128], F32, kind="ExternalInput").ap()
    sinT = nc.dram_tensor("sinT", [128, GT * 128], F32, kind="ExternalInput").ap()
    tri = nc.dram_tensor("tri", [128, 128], BF16, kind="ExternalInput").ap()
    onesc = nc.dram_tensor("onesc", [128, 128], BF16, kind="ExternalInput").ap()
    identb = nc.dram_tensor("identb", [128, 128], BF16, kind="ExternalInput").ap()
    cofs = nc.dram_tensor("cofs", [1, 2], I32, kind="ExternalInput").ap()

    out = nc.dram_tensor("out", [TPC, D], F32, kind="ExternalOutput").ap()

    cin_xn = nc.dram_tensor("cin_xn", [D, TPC], BF16, kind="Internal").ap()
    cout_xn = nc.dram_tensor("cout_xn", [NCORE * D, TPC], BF16, kind="Internal", addr_space="Shared").ap()
    cin_c = nc.dram_tensor("cin_c", [512, T], BF16, kind="Internal").ap()
    cout_c = nc.dram_tensor("cout_c", [NCORE * 512, T], BF16, kind="Internal", addr_space="Shared").ap()

    dbg_out = {}
    dbg_shapes = {"d_xn": [128, NDC * TPC], "d_q": [128, NQH * T],
                  "d_k": [128, T], "d_v": [128, T], "d_ctx": [512, T],
                  "d_e01": [64, TPC], "d_v01": [64, D], "d_w01": [128, 1280]}
    for name in dbg:
        dbg_out[name] = nc.dram_tensor(name, dbg_shapes[name], F32,
                                       kind="ExternalOutput").ap()

    base0 = (nc.sbuf_base + 31) // 32 * 32
    top = nc.sbuf_top
    off = [base0]

    def place(name, shape, dtype, at=None):
        nbytes = int(np.prod(shape[1:])) * dtype.itemsize if hasattr(dtype, "itemsize") else None
        size = shape[1] * {F32: 4, BF16: 2, U8: 1, I32: 4}[dtype]
        pos = off[0] if at is None else at
        t = nc.alloc_sbuf_tensor_at(name, shape, dtype, offset=pos)
        if at is None:
            off[0] = (pos + size + 31) // 32 * 32
        return t

    xn_fm = place("xn_fm", [128, NDC * TPC], BF16)
    regB = place("regB", [128, 32768], BF16)
    k_res = place("k_res", [128, T], BF16)
    v_res = place("v_res", [128, T], BF16)
    v01b = place("v01b", [128, D], BF16)
    e01b = place("e01b", [128, TPC], BF16)
    tri_sb = place("tri_sb", [128, 128], BF16)
    ones_sb = place("ones_sb", [128, 128], BF16)
    id_sb = place("id_sb", [128, 128], BF16)
    cofs_sb = place("cofs_sb", [1, 8], I32)
    R = off[0]
    assert top - R >= 36000, f"R too small: {top - R}"

    regB_at = regB.manual_sbuf_range[0]
    xnt = place("xnt", [128, D], BF16, at=regB_at)
    xbuf = place("xbuf", [128, D], F32, at=regB_at + 8192)
    bnbuf = place("bnbuf", [128, 48], F32, at=regB_at + 24576)
    stats = place("stats", [128, 8], F32, at=regB_at + 24800)
    mtb = [place(f"mtb{i}", [128, 512], BF16, at=R + 20640 + 1024 * i) for i in range(8)]
    cf_sb = place("cf_sb", [128, 1280], F32, at=R)
    mk_sb = place("mk_sb", [128, 1280], F32, at=R + 5152)
    enc_sb = place("enc_sb", [128, NDC * 40], BF16, at=R + 18048)
    w01_f = place("w01_f", [128, NDC * 40], F32, at=R + 10304)
    w01_b = place("w01_b", [128, NDC * 40], BF16, at=R + 15456)
    xnw = [place(f"xnw{i}", [128, 1024], BF16, at=R + 2048 * i) for i in range(3)]
    wqw = [place(f"wqw{i}", [128, 1024], BF16, at=R + 6144 + 2048 * i) for i in range(3)]
    wkw = [place(f"wkw{i}", [128, 512], BF16, at=R + 12288 + 1024 * i) for i in range(3)]
    scrq = place("scrq", [128, 512], F32, at=R + 15360)
    scrq2 = place("scrq2", [128, 512], F32, at=R + 17408)
    scrb = [place(f"scrb{i}", [128, 512], BF16, at=R + 19456 + 1024 * i) for i in range(2)]
    kscr = [place(f"kscr{i}", [128, 128], BF16, at=R + 21504 + 256 * i) for i in range(2)]
    ropc = [place(f"ropc{i}", [128, 512], F32, at=R + 22016 + 2048 * i) for i in range(2)]
    rops = [place(f"rops{i}", [128, 512], F32, at=R + 26112 + 2048 * i) for i in range(2)]
    pt_sb = [place(f"pt_sb{i}", [128, 512], BF16, at=R + 1024 * i) for i in range(4)]
    rsr_sb = [place(f"rsr{i}", [128, 512], F32, at=R + 4096 + 2048 * i) for i in range(2)]
    ctxb = [place(f"ctxb{i}", [128, 512], BF16, at=R + 8192 + 1024 * i) for i in range(2)]
    w6 = [place(f"w6_{i}", [128, 512], BF16, at=R + 12288 + 1024 * i) for i in range(8)]
    xld = [place(f"xld{i}", [128, 512], F32, at=R + 4096 + 2048 * i) for i in range(2)]
    outsb = [place(f"outsb{i}", [128, 512], F32, at=R + 8192 + 2048 * i) for i in range(2)]

    ps = [nc.alloc_psum_tensor(f"ps{i}", [128, 512], F32) for i in range(6)]
    psb_guard = [nc.psum_tensor(f"psb{i}", [128, 640], BF16) for i in range(2)]
    psb = [g.__enter__() for g in psb_guard]

    plan = Plan(nc)
    E, S, n = plan.emit, plan.sem, plan.n
    for s in (["s_ld", "xb", "cfl", "enl", "dbgl", "pe", "pe_s", "pe_pv",
               "ac", "scp", "dv", "dvm", "cc", "stc", "sto", "co_ld",
               "bb_ld", "cx_ld", "cst0", "cst1", "so0", "so1", "xl0", "xl1"]
              + [f"mtb{i}" for i in range(8)] + [f"xnw{i}" for i in range(3)]
              + [f"wv{i}" for i in range(3)] + [f"rop{i}" for i in range(2)]
              + [f"w6s{i}" for i in range(8)]):
        S(s)

    E("sync", lambda e: e.dma_start(tri_sb[:], tri), inc="s_ld", inc_amt=16)
    E("sync", lambda e: e.dma_start(ones_sb[:], onesc), inc="s_ld", inc_amt=16)
    E("sync", lambda e: e.dma_start(id_sb[:], identb), inc="s_ld", inc_amt=16)
    E("sync", lambda e: e.dma_start(cofs_sb[:, 0:2], cofs), inc="co_ld", inc_amt=16)
    E("gpsimd", lambda e: e.dma_start(v01b[40:41, :], bbt), inc="bb_ld", inc_amt=16)

    xc_t = xc.rearrange("(t p) c -> t p c", p=128)
    xn_v = xn_fm.ap().rearrange("p (d t) -> p d t", t=TPC)
    p1_ts_dv = []
    p1_tr_pe = []
    p1_evac_scp = []
    for tt in range(NTT):
        w = [("dv", p1_ts_dv[tt - 1])] if tt >= 1 else []
        E("sync", lambda e, tt=tt: e.dma_start(xbuf[:], xc_t[tt]),
          waits=w, inc="xb", inc_amt=16)
        sld = (tt + 1) * 16
        for k in range(8):
            wk = [("xb", sld)] if k == 0 else []
            E("vector", lambda e, k=k: e.bn_stats(
                bnbuf[:, 6 * k:6 * k + 6], xbuf[:, 512 * k:512 * k + 512]),
              waits=wk, inc="dv" if k == 7 else None)
        E("vector", lambda e: e.bn_aggr(stats[:, 0:2], bnbuf[:]),
          waits=[("dv", n("dv"))], inc="dv")
        E("vector", lambda e: e.tensor_single_scalar(
            stats[:, 4:5], stats[:, 1:2], 1e-5, ALU.add),
          waits=[("dv", n("dv"))], inc="dv")
        E("scalar", lambda e: e.activation(stats[:, 2:3], stats[:, 4:5],
                                           AF.Ln),
          waits=[("dv", n("dv"))], inc="scp")
        E("scalar", lambda e: e.activation(stats[:, 3:4], stats[:, 2:3],
                                           AF.Exp, scale=-0.5),
          waits=[("scp", n("scp"))], inc="scp")
        w = [("scp", n("scp"))]
        if tt >= 1:
            w.append(("pe", p1_tr_pe[tt - 1]))
        E("vector", lambda e: e.tensor_scalar(
            xnt[:], xbuf[:], stats[:, 0:1], stats[:, 3:4],
            ALU.subtract, ALU.mult), waits=w, inc="dv")
        p1_ts_dv.append(n("dv"))
        for g4 in range(8):
            pb = (tt * 8 + g4) % 2
            for j in range(4):
                dchunk = g4 * 4 + j
                w = []
                if j == 0:
                    w.append(("dv", p1_ts_dv[tt]))
                    if tt == 0 and g4 == 0:
                        w.append(("s_ld", 3 * 16))
                    if len(p1_evac_scp) >= 2:
                        w.append(("scp", p1_evac_scp[-2]))
                E("tensor", lambda e, pb=pb, j=j, dchunk=dchunk: e.transpose(
                    psb[pb][:, 128 * j:128 * j + 128],
                    xnt[:, 128 * dchunk:128 * dchunk + 128], id_sb[:]),
                  waits=w, inc="pe" if j == 3 else None)
            E("scalar", lambda e, pb=pb, g4=g4, tt=tt: e.copy(
                xn_v[:, g4 * 4:g4 * 4 + 4, tt * 128:tt * 128 + 128],
                psb[pb].ap()[:, 0:512].rearrange("p (j c) -> p j c", j=4)),
              waits=[("pe", n("pe"))], inc="scp")
            p1_evac_scp.append(n("scp"))
        p1_tr_pe.append(n("pe"))
    xn_done_scp = n("scp")

    cin_xn_t = cin_xn.rearrange("(d p) c -> d p c", p=128)
    for dchunk in range(NDC):
        E("scalar", lambda e, dchunk=dchunk: e.dma_start(
            cin_xn_t[dchunk], xn_fm[:, dchunk * TPC:(dchunk + 1) * TPC]),
          waits=[("scp", xn_done_scp)] if dchunk == 0 else [],
          inc="stc", inc_amt=16)
    E("gpsimd", lambda e: e.collective_compute(
        "AllGather", ALU.bypass, replica_groups=[list(range(NCORE))],
        ins=[cin_xn.opt()], outs=[cout_xn.opt()]),
      waits=[("stc", n("stc"))], inc="cc")

    if "d_xn" in dbg_out:
        for i in range(NDC * TPC // 512):
            E("gpsimd", lambda e, i=i: e.dma_start(
                dbg_out["d_xn"][:, i * 512:(i + 1) * 512],
                xn_fm[:, i * 512:(i + 1) * 512]),
              waits=[("scp", xn_done_scp)], inc="dbgl", inc_amt=16)

    if upto == "p1":
        with nc.Block() as blk:
            plan.build(blk)
        return nc

    p3_dv_acc = 0
    for ei in range(3):
        w = [] if ei == 0 else [("dv", p3_dv_acc)]
        E("gpsimd", lambda e, ei=ei: e.dma_start(cf_sb[:, 0:1024], c0r[ei]),
          waits=w, inc="cfl", inc_amt=16)
        E("gpsimd", lambda e, ei=ei: e.dma_start(cf_sb[:, 1024:1280], c1r[ei]),
          inc="cfl", inc_amt=16)
        E("gpsimd", lambda e, ei=ei: e.dma_start(mk_sb[:, 0:1024], m0r[ei]),
          inc="cfl", inc_amt=16)
        E("gpsimd", lambda e, ei=ei: e.dma_start(mk_sb[:, 1024:1280], m1r[ei]),
          inc="cfl", inc_amt=16)
        gl = n("cfl")
        if ei == 0:
            E("vector", lambda e: e.tensor_tensor(w01_f[:, 0:1280], cf_sb[:],
                                                  mk_sb[:], ALU.mult),
              waits=[("cfl", gl)], inc="dv")
        else:
            E("vector", lambda e: e.tensor_tensor(mk_sb[:], cf_sb[:], mk_sb[:],
                                                  ALU.mult),
              waits=[("cfl", gl)], inc="dv")
            E("vector", lambda e: e.tensor_tensor(w01_f[:, 0:1280],
                                                  w01_f[:, 0:1280], mk_sb[:],
                                                  ALU.add),
              waits=[("dv", n("dv"))], inc="dv")
        p3_dv_acc = n("dv")
    w01bv = w01_b.ap().rearrange("p (ch r) -> p ch r", r=40)
    E("vector", lambda e: e.tensor_copy(
        w01bv[:, :, 0:32],
        w01_f[:, 0:1024].rearrange("p (ch r) -> p ch r", r=32)),
      waits=[("dv", p3_dv_acc)], inc="dv")
    E("vector", lambda e: e.tensor_copy(
        w01bv[:, :, 32:40],
        w01_f[:, 1024:1280].rearrange("p (ch r) -> p ch r", r=8)),
      inc="dv")
    w01_ready = n("dv")
    E("gpsimd", lambda e: e.dma_start(enc_sb[:], enc01T.rearrange(
        "(ch p) r -> p ch r", p=128)), inc="enl", inc_amt=16)
    E("vector", lambda e: e.memset(e01b[32:64, :], 1.0), inc="dv")
    e01_ones_dv = n("dv")

    mt_r = mt.rearrange("(ch p) c -> ch p c", p=128)
    mt_grp = []
    v01_scp = []
    mload = 0
    mtb_cnt = [0] * 8
    for jb in range(8):
        for ch in range(NDC):
            ring = mload % 8
            w = []
            if mload >= 8:
                w.append(("pe", mt_grp[mload - 8]))
            E("gpsimd", lambda e, ch=ch, jb=jb, ring=ring: e.dma_start(
                mtb[ring][:], mt_r[ch][:, jb * 512:(jb + 1) * 512]),
              waits=w, inc=f"mtb{ring}", inc_amt=16)
            mload += 1
            mtb_cnt[ring] += 1
            w = [(f"mtb{ring}", mtb_cnt[ring] * 16)]
            if ch == 0:
                w.append(("dv", w01_ready))
                if jb >= 2:
                    w.append(("scp", v01_scp[jb - 2]))
            E("tensor", lambda e, ch=ch, jb=jb, ring=ring: e.matmul(
                ps[4 + jb % 2][0:40, :], w01_b[:, ch * 40:(ch + 1) * 40],
                mtb[ring][:], start=(ch == 0), stop=(ch == NDC - 1)),
              waits=w, inc="pe")
            mt_grp.append(n("pe"))
        E("scalar", lambda e, jb=jb: e.mul(
            v01b[0:40, jb * 512:(jb + 1) * 512], ps[4 + jb % 2][0:40, :], 0.5),
          waits=[("pe", n("pe"))], inc="scp")
        v01_scp.append(n("scp"))


    for tc in range(2):
        for ch in range(NDC):
            w = []
            if ch == 0:
                w = [("enl", 16), ("scp", v01_scp[tc + 6]),
                     ("scp", xn_done_scp)]
            E("tensor", lambda e, ch=ch, tc=tc: e.matmul(
                ps[4 + tc][0:40, :], enc_sb[:, ch * 40:(ch + 1) * 40],
                xn_fm[:, ch * TPC + tc * 512: ch * TPC + tc * 512 + 512],
                start=(ch == 0), stop=(ch == NDC - 1)),
              waits=w, inc="pe" if ch == NDC - 1 else None)
        E("scalar", lambda e, tc=tc: e.copy(
            e01b[0:40, tc * 512:(tc + 1) * 512], ps[4 + tc][0:40, :]),
          waits=[("pe", n("pe")), ("dv", e01_ones_dv)], inc="scp")
    p3b_done_scp = n("scp")
    p3b_done_pe = n("pe")

    if "d_w01" in dbg_out:
        E("gpsimd", lambda e: e.dma_start(dbg_out["d_w01"], w01_b[:]),
          waits=[("dv", w01_ready)], inc="dbgl", inc_amt=16)
    if "d_v01" in dbg_out:
        for i in range(8):
            E("gpsimd", lambda e, i=i: e.dma_start(
                dbg_out["d_v01"][0:64, i * 512:(i + 1) * 512],
                v01b[0:64, i * 512:(i + 1) * 512]),
              waits=[("scp", p3b_done_scp)], inc="dbgl", inc_amt=16)
    if "d_e01" in dbg_out:
        for i in range(2):
            E("gpsimd", lambda e, i=i: e.dma_start(
                dbg_out["d_e01"][0:64, i * 512:(i + 1) * 512],
                e01b[0:64, i * 512:(i + 1) * 512]),
              waits=[("scp", p3b_done_scp)], inc="dbgl", inc_amt=16)

    if upto == "p3b":
        with nc.Block() as blk:
            plan.build(blk)
        return nc

    cout_r = cout_xn.rearrange("(r d p) c -> r d p c", p=128, d=NDC)
    wq_r = wq_my.rearrange("(d p) c -> d p c", p=128)
    wkv_r = wkv_my.rearrange("(d p) c -> d p c", p=128)
    q_fm = regB.ap().rearrange("p (h t) -> p h t", h=NQH)

    grp_pe = []
    rop_cnt = [0, 0]
    xnw_cnt = [0] * 3
    wv_cnt = [0] * 3
    ld_i = 0
    q_dv = {}
    kv_dv = {}
    v_scp = {}
    evac_scp = []
    qtr_pe = {}
    ktr_pe = {}
    tab_tch_dv = []

    for tch in range(NTCH):
        r, sl = tch // 2, tch % 2
        tpar = tch % 2
        w = []
        if tch < 2:
            w.append(("pe", p3b_done_pe))
        else:
            w.append(("dv", tab_tch_dv[tch - 2]))
        E("sync", lambda e, tch=tch, tpar=tpar: e.dma_start(
            ropc[tpar][:], cosT[:, tch * 512:(tch + 1) * 512]),
          waits=w, inc=f"rop{tpar}", inc_amt=16)
        E("sync", lambda e, tch=tch, tpar=tpar: e.dma_start(
            rops[tpar][:], sinT[:, tch * 512:(tch + 1) * 512]),
          inc=f"rop{tpar}", inc_amt=16)
        rop_cnt[tpar] += 2
        tab_par, tab_here = tpar, rop_cnt[tpar] * 16

        for dp in range(16):
            ring = ld_i % 3
            w = []
            if ld_i == 0:
                w.append(("cc", 1))
            if ld_i < 3:
                w.append(("pe", p3b_done_pe))
            if ld_i >= 3:
                w.append(("pe", grp_pe[ld_i - 3]))
            E("sync", lambda e, r=r, sl=sl, dp=dp, ring=ring: e.dma_start(
                xnw[ring].ap().rearrange("p (d c) -> p d c", d=2),
                cout_r[r][dp * 2:dp * 2 + 2, :, sl * 512:(sl + 1) * 512]
                .rearrange("d p c -> p d c")),
              waits=w, inc=f"xnw{ring}", inc_amt=16)
            wg = [("pe", grp_pe[ld_i - 3] if ld_i >= 3 else p3b_done_pe)]
            E("gpsimd", lambda e, dp=dp, ring=ring: e.dma_start(
                wqw[ring].ap().rearrange("p (d c) -> p d c", d=2),
                wq_r[dp * 2:dp * 2 + 2, :, :].rearrange("d p c -> p d c")),
              waits=wg, inc=f"wv{ring}", inc_amt=16)
            E("gpsimd", lambda e, dp=dp, ring=ring: e.dma_start(
                wkw[ring].ap().rearrange("p (d c) -> p d c", d=2),
                wkv_r[dp * 2:dp * 2 + 2, :, :].rearrange("d p c -> p d c")),
              inc=f"wv{ring}", inc_amt=16)
            ld_i += 1
            xnw_cnt[ring] += 1
            wv_cnt[ring] += 2
            xw_here, w2_here = xnw_cnt[ring] * 16, wv_cnt[ring] * 16

            for dd in range(2):
                for tt in range(4):
                    gtile = tch * 4 + tt
                    w = []
                    if dd == 0 and tt == 0:
                        w = [(f"xnw{ring}", xw_here), (f"wv{ring}", w2_here)]
                    if dp == 0 and dd == 0 and (gtile - 4) in q_dv:
                        w.append(("dv", q_dv[gtile - 4]))
                    E("tensor", lambda e, dd=dd, tt=tt, ring=ring, dp=dp: e.matmul(
                        ps[tt][:], xnw[ring][:, dd * 512 + tt * 128: dd * 512 + tt * 128 + 128],
                        wqw[ring][:, dd * 512:(dd + 1) * 512],
                        start=(dp == 0 and dd == 0), stop=(dp == 15 and dd == 1),
                        skip_group_check=True), waits=w)
                for tt in range(4):
                    gtile = tch * 4 + tt
                    kvbk, kvc = 4 + tt // 2, (tt % 2) * 256
                    w = []
                    if dp == 0 and dd == 0 and tt % 2 == 0:
                        if tch == 0:
                            w.append(("scp", p3b_done_scp))
                        for gprev in (gtile - 4, gtile - 3):
                            if gprev in kv_dv:
                                w.append(("dv", kv_dv[gprev]))
                                w.append(("scp", v_scp[gprev]))
                    E("tensor", lambda e, dd=dd, tt=tt, ring=ring, kvbk=kvbk, kvc=kvc, dp=dp: e.matmul(
                        ps[kvbk][:, kvc:kvc + 256],
                        xnw[ring][:, dd * 512 + tt * 128: dd * 512 + tt * 128 + 128],
                        wkw[ring][:, dd * 256:(dd + 1) * 256],
                        start=(dp == 0 and dd == 0 and tt % 2 == 0),
                        stop=(dp == 15 and dd == 1),
                        skip_group_check=True),
                      waits=w, inc="pe" if (dd == 1 and tt == 3) else None)
            grp_pe.append(n("pe"))
        tch_mm_pe = n("pe")

        for tt in range(4):
            gtile = tch * 4 + tt
            spar = gtile % 2
            kvbk, kvc = 4 + tt // 2, (tt % 2) * 256
            tcol = tt * 128
            cosb = ropc[tpar].ap().rearrange(
                "p (a f) -> p a f", a=4)[:, tt:tt + 1, :].broadcast_to((128, 4, 128))
            sinb = rops[tpar].ap().rearrange(
                "p (a f) -> p a f", a=4)[:, tt:tt + 1, :].broadcast_to((128, 4, 128))
            cos2 = ropc[tpar].ap().rearrange(
                "p (a h f) -> p a h f", a=4, h=2)[:, tt:tt + 1, :, :].broadcast_to((128, 4, 2, 64))
            sin2 = rops[tpar].ap().rearrange(
                "p (a h f) -> p a h f", a=4, h=2)[:, tt:tt + 1, :, :].broadcast_to((128, 4, 2, 64))
            qv = ps[tt].ap().rearrange("p (b h f) -> p b h f", b=4, h=2)
            s2v = scrq2.ap().rearrange("p (b h f) -> p b h f", b=4, h=2)
            w = [("pe", tch_mm_pe), (f"rop{tab_par}", tab_here)]
            E("vector", lambda e, qv=qv, s2v=s2v, sin2=sin2: e.tensor_tensor(
                s2v[:, :, 0, :], qv[:, :, 1, :], sin2[:, :, 0, :], ALU.mult),
              waits=w, inc="dv")
            E("vector", lambda e, qv=qv, s2v=s2v, sin2=sin2: e.tensor_tensor(
                s2v[:, :, 1, :], qv[:, :, 0, :], sin2[:, :, 1, :], ALU.mult),
              inc="dv")
            E("vector", lambda e, tt=tt, cosb=cosb: e.tensor_tensor(
                scrq.ap().rearrange("p (b f) -> p b f", b=4),
                ps[tt].ap().rearrange("p (b f) -> p b f", b=4), cosb, ALU.mult),
              inc="dv")
            q_dv[gtile] = n("dv")
            w = [("dv", n("dv"))]
            if spar in qtr_pe:
                w.append(("pe", qtr_pe[spar]))
            E("vector", lambda e, spar=spar: e.tensor_tensor(
                scrb[spar][:], scrq[:], scrq2[:], ALU.add), waits=w, inc="dv")
            scrb_dv = n("dv")
            kv2 = ps[kvbk].ap()[:, kvc:kvc + 128].rearrange(
                "p (h f) -> p h f", h=2)
            s2k = scrq2.ap()[:, 0:128].rearrange("p (h f) -> p h f", h=2)
            c1b = ropc[tpar].ap().rearrange(
                "p (a h f) -> p a h f", a=4, h=2)[:, tt, :, :]
            s1b = rops[tpar].ap().rearrange(
                "p (a h f) -> p a h f", a=4, h=2)[:, tt, :, :]
            E("vector", lambda e, kv2=kv2, s2k=s2k, s1b=s1b: e.tensor_tensor(
                s2k[:, 0, :], kv2[:, 1, :], s1b[:, 0, :], ALU.mult),
              waits=[("dv", scrb_dv)], inc="dv")
            E("vector", lambda e, kv2=kv2, s2k=s2k, s1b=s1b: e.tensor_tensor(
                s2k[:, 1, :], kv2[:, 0, :], s1b[:, 1, :], ALU.mult), inc="dv")
            E("vector", lambda e, kvbk=kvbk, kvc=kvc, c1b=c1b: e.tensor_tensor(
                scrq.ap()[:, 0:128].rearrange("p (h f) -> p h f", h=2),
                ps[kvbk].ap()[:, kvc:kvc + 128].rearrange("p (h f) -> p h f", h=2),
                c1b, ALU.mult), inc="dv")
            kv_dv[gtile] = n("dv")
            if tt == 3:
                tab_tch_dv.append(n("dv"))
            w = [("dv", n("dv"))]
            if spar in ktr_pe:
                w.append(("pe", ktr_pe[spar]))
            E("vector", lambda e, spar=spar: e.tensor_tensor(
                kscr[spar][:], scrq[:, 0:128], scrq2[:, 0:128], ALU.add),
              waits=w, inc="dv")
            kscr_dv = n("dv")
            E("scalar", lambda e, kvbk=kvbk, kvc=kvc, gtile=gtile: e.copy(
                v_res[:, gtile * 128:(gtile + 1) * 128],
                ps[kvbk][:, kvc + 128:kvc + 256]),
              waits=[("pe", tch_mm_pe)], inc="scp")
            v_scp[gtile] = n("scp")
            for h in range(NQH):
                w = []
                if h == 0:
                    w = [("dv", scrb_dv)]
                    if len(evac_scp) >= 2:
                        w.append(("scp", evac_scp[-2]))
                E("tensor", lambda e, h=h, spar=spar: e.transpose(
                    psb[spar][:, h * 128:(h + 1) * 128],
                    scrb[spar][:, h * 128:(h + 1) * 128], id_sb[:]),
                  waits=w, inc="pe" if h == 3 else None)
            qtr_pe[spar] = n("pe")
            E("tensor", lambda e, spar=spar: e.transpose(
                psb[spar][:, 512:640], kscr[spar][:], id_sb[:]),
              waits=[("dv", kscr_dv)], inc="pe")
            ktr_pe[spar] = n("pe")
            E("scalar", lambda e, spar=spar, gtile=gtile: e.copy(
                q_fm[:, :, gtile * 128:(gtile + 1) * 128],
                psb[spar].ap()[:, 0:512].rearrange("p (h c) -> p h c", h=4)),
              waits=[("pe", n("pe"))], inc="scp")
            E("scalar", lambda e, spar=spar, gtile=gtile: e.copy(
                k_res[:, gtile * 128:(gtile + 1) * 128], psb[spar][:, 512:640]),
              inc="scp")
            evac_scp.append(n("scp"))
    p2_done_scp = n("scp")

    for nm, buf, wd in (("d_q", regB, NQH * T), ("d_k", k_res, T), ("d_v", v_res, T)):
        if nm in dbg_out:
            for i in range(wd // 512):
                E("gpsimd", lambda e, i=i, nm=nm, buf=buf: e.dma_start(
                    dbg_out[nm][:, i * 512:(i + 1) * 512],
                    buf[:, i * 512:(i + 1) * 512]),
                  waits=[("scp", p2_done_scp)], inc="dbgl", inc_amt=16)

    if upto == "p2":
        with nc.Block() as blk:
            plan.build(blk)
        return nc

    for g in reversed(psb_guard):
        g.__exit__(None, None, None)
    ps6 = [nc.alloc_psum_tensor(f"ps6_{i}", [128, 512], F32) for i in range(2)]
    p6b = ps + ps6
    units = []
    gid = 0
    for hq in range(NQH):
        for qch in range(NTCH):
            nkt = (qch + 1) * 4
            for kt in range(nkt):
                diag = kt >= qch * 4
                qoff = (kt - qch * 4) * 128 if diag else 0
                units.append((hq, qch, kt, diag, qoff, 512 - qoff,
                              hq * T + qch * 512 + qoff, gid, kt == nkt - 1))
            gid += 1
    NU = len(units)
    ctx_of = lambda g: (ps6[0] if g % 2 else ps[4])
    rs_of = lambda g: (ps6[1] if g % 2 else ps[5])

    mask_cnt = [0]
    mask_idx = {}
    ev_dv = []
    grp_pv = {}
    cst_cnt = [0, 0]

    def emit_S(u):
        hq, qch, kt, diag, qoff, N, qcol, g, last = units[u]
        w = []
        if u == 0:
            w.append(("scp", p2_done_scp))
        if u >= 4:
            w.append(("ac", u - 3))
        E("tensor", lambda e, kt=kt, qcol=qcol, qoff=qoff, N=N, sbk=u % 4: e.matmul(
            ps[sbk][:, qoff:qoff + N], k_res[:, kt * 128:(kt + 1) * 128],
            regB[:, qcol:qcol + N], start=True, stop=True),
          waits=w, inc="pe_s")

    def emit_exp(u):
        hq, qch, kt, diag, qoff, N, qcol, g, last = units[u]
        w = [("pe_s", u + 1)]
        if u >= 4:
            w.append(("pe_pv", u - 3))
        E("scalar", lambda e, sbk=u % 4, qoff=qoff, N=N, u=u: e.activation(
            pt_sb[u % 4][:, qoff:qoff + N], ps[sbk][:, qoff:qoff + N],
            AF.Exp, scale=SCALE),
          waits=w, inc="ac")
        if diag:
            E("vector", lambda e, u=u, qoff=qoff: e.tensor_tensor(
                pt_sb[u % 4][:, qoff:qoff + 128],
                pt_sb[u % 4][:, qoff:qoff + 128], tri_sb[:], ALU.mult),
              waits=[("ac", u + 1)], inc="dvm")
            mask_cnt[0] += 1
            mask_idx[u] = mask_cnt[0]

    def emit_rspv(u):
        hq, qch, kt, diag, qoff, N, qcol, g, last = units[u]
        w = [("ac", u + 1)]
        if diag:
            w.append(("dvm", mask_idx[u]))
        if kt == 0 and g >= 2:
            w.append(("dv", ev_dv[g - 2]))
        E("tensor", lambda e, u=u, qoff=qoff, N=N, kt=kt, g=g, last=last: e.matmul(
            rs_of(g)[:, qoff:qoff + N], ones_sb[:],
            pt_sb[u % 4][:, qoff:qoff + N],
            start=(kt == 0), stop=last, skip_group_check=True), waits=w)
        E("tensor", lambda e, u=u, qoff=qoff, N=N, kt=kt, g=g, last=last: e.matmul(
            ctx_of(g)[:, qoff:qoff + N], v_res[:, kt * 128:(kt + 1) * 128],
            pt_sb[u % 4][:, qoff:qoff + N],
            start=(kt == 0), stop=last, skip_group_check=True), inc="pe_pv")
        if last:
            grp_pv[g] = n("pe_pv")
            emit_evac(g, hq, qch)

    def emit_evac(g, hq, qch):
        epar = g % 2
        w = [("pe_pv", grp_pv[g])]
        if g >= 2:
            w.append(("dv", ev_dv[g - 2]))
        E("vector", lambda e, epar=epar, g=g: e.reciprocal(
            rsr_sb[epar][:], rs_of(g)[:]), waits=w, inc="dv")
        w = [("dv", n("dv"))]
        if g >= 2:
            w.append((f"cst{epar}", cst_cnt[epar] * 16))
        E("vector", lambda e, epar=epar, g=g: e.tensor_tensor(
            ctxb[epar][:], ctx_of(g)[:], rsr_sb[epar][:], ALU.mult),
          waits=w, inc="dv")
        ev_dv.append(n("dv"))
        E("scalar", lambda e, epar=epar, hq=hq, qch=qch: e.dma_start(
            cin_c[hq * 128:(hq + 1) * 128, qch * 512:(qch + 1) * 512],
            ctxb[epar][:]),
          waits=[("dv", n("dv"))], inc=f"cst{epar}", inc_amt=16)
        cst_cnt[epar] += 1

    for u in range(NU):
        emit_S(u)
        emit_exp(u)
        if u >= 2:
            emit_rspv(u - 2)
    emit_rspv(NU - 2)
    emit_rspv(NU - 1)
    p4_ac = n("ac")
    p4_pv = n("pe_pv")
    p4_dv_last = ev_dv[-1]

    wb_r = wbT.rearrange("(d p) c -> d p c", p=128)
    w6_cnt = [0] * 8
    for d in range(8):
        E("gpsimd", lambda e, d=d: e.dma_start(
            w6[d][:], wb_r[d][:, 0:512]),
          waits=[("scp", p2_done_scp)] if d == 0 else [],
          inc=f"w6s{d}", inc_amt=16)
        w6_cnt[d] += 1

    E("gpsimd", lambda e: e.collective_compute(
        "AllGather", ALU.bypass, replica_groups=[list(range(NCORE))],
        ins=[cin_c.opt()], outs=[cout_c.opt()]),
      waits=[("cst0", cst_cnt[0] * 16), ("cst1", cst_cnt[1] * 16)], inc="cc")

    if "d_ctx" in dbg_out:
        for i in range(4):
            E("gpsimd", lambda e, i=i: e.dma_start(
                dbg_out["d_ctx"][i * 128:(i + 1) * 128, :],
                cin_c[i * 128:(i + 1) * 128, :]),
              waits=[("cc", 2)], inc="dbgl", inc_amt=16)

    if upto == "p4":
        with nc.Block() as blk:
            plan.build(blk)
        return nc


    cout_c_p = cout_c.rearrange("(d p) c -> p d c", p=128)

    def load_ctx_branched(e):
        last = {}
        with e.register("cid") as cid:
            e.reg_load(cid, cofs_sb[0:1, 1:2])
            for k in range(NCORE):
                with e.If_eq(cid, k):
                    e.dma_start(
                        regB.ap().rearrange("p (d t) -> p d t", t=TPC),
                        cout_c_p[:, :, k * TPC:(k + 1) * TPC]
                    ).then_inc(plan.handles["cx_ld"], 16)
            e.end_ifs()
        return None

    S("cx_ld")
    plan.counts["cx_ld"] += 16
    plan.items["sync"].append((load_ctx_branched,
                               [(S("cc"), 2), (S("co_ld"), 16)], None, 1))
    ctx_ld = n("cx_ld")

    wo_r = wo.rearrange("(d p) c -> d p c", p=128)
    outv = out.rearrange("(t p) c -> t p c", p=128)

    wgrp_pe = []
    w6i = 0
    p6_ev_dv = []
    nst = 0
    xl_cnt = [0, 0]
    so_cnt = [0, 0]
    for ob in range(8):
        for si, (wsrc, lhs) in enumerate(((wb_r, xn_fm), (wo_r, regB))):
            for d in range(NDC):
                ring = w6i % 8
                if w6i >= 8:
                    w = []
                    if w6i >= 8:
                        w.append(("pe", wgrp_pe[w6i - 8]))
                    E("gpsimd", lambda e, d=d, ob=ob, ring=ring, wsrc=wsrc: e.dma_start(
                        w6[ring][:], wsrc[d][:, ob * 512:(ob + 1) * 512]),
                      waits=w, inc=f"w6s{ring}", inc_amt=16)
                    w6_cnt[ring] += 1
                w6i += 1
                wld = w6_cnt[ring] * 16
                for tt in range(NTT):
                    w = []
                    if tt == 0:
                        w = [(f"w6s{ring}", wld)]
                    if si == 0 and d == 0:
                        if ob == 0:
                            if tt == 0:
                                w.append(("ac", p4_ac))
                                w.append(("scp", p3b_done_scp))
                                w.append(("bb_ld", 16))
                            if tt == 4:
                                w.append(("dv", p4_dv_last))
                        else:
                            w.append(("dv", p6_ev_dv[(ob - 1) * 8 + tt]))
                    if si == 1 and d == 0 and tt == 0 and ob == 0:
                        w.append(("cx_ld", ctx_ld))
                    first = (si == 0 and d == 0)
                    E("tensor", lambda e, d=d, tt=tt, ring=ring, lhs=lhs, first=first: e.matmul(
                        p6b[tt][:],
                        lhs[:, d * TPC + tt * 128: d * TPC + tt * 128 + 128],
                        w6[ring][:], start=first, stop=False,
                        skip_group_check=True),
                      waits=w, inc="pe" if tt == NTT - 1 else None)
                wgrp_pe.append(n("pe"))
        for tt in range(NTT):
            E("tensor", lambda e, tt=tt, ob=ob: e.matmul(
                p6b[tt][:], e01b[0:41, tt * 128:(tt + 1) * 128],
                v01b[0:41, ob * 512:(ob + 1) * 512],
                start=False, stop=True, skip_group_check=True),
              inc="pe_pv")
            pv_here = n("pe_pv")
            xpar = (ob * NTT + tt) % 2
            w = []
            if len(p6_ev_dv) >= 2:
                w.append(("dv", p6_ev_dv[-2]))
            elif ob == 0 and tt < 2:
                w.append(("dv", p4_dv_last))
            E("sync", lambda e, ob=ob, tt=tt, xpar=xpar: e.dma_start(
                xld[xpar][:], xc_t[tt][:, ob * 512:(ob + 1) * 512]),
              waits=w, inc=f"xl{xpar}", inc_amt=16)
            xl_cnt[xpar] += 1
            w = [("pe_pv", pv_here), (f"xl{xpar}", xl_cnt[xpar] * 16)]
            if nst >= 2:
                w.append((f"so{nst % 2}", so_cnt[nst % 2] * 16))
            elif nst < 2:
                w.append((f"cst{nst % 2}", cst_cnt[nst % 2] * 16))
            E("vector", lambda e, tt=tt, xpar=xpar, opar=nst % 2: e.tensor_tensor(
                outsb[opar][:], p6b[tt][:], xld[xpar][:], ALU.add),
              waits=w, inc="dv")
            p6_ev_dv.append(n("dv"))
            E("scalar", lambda e, tt=tt, ob=ob, opar=nst % 2: e.dma_start(
                outv[tt][:, ob * 512:(ob + 1) * 512], outsb[opar][:]),
              waits=[("dv", n("dv"))], inc=f"so{nst % 2}", inc_amt=16)
            so_cnt[nst % 2] += 1
            nst += 1

    E("sync", lambda e: None, waits=[("so0", so_cnt[0] * 16),
                                     ("so1", so_cnt[1] * 16)])
    E("gpsimd", lambda e: None, waits=[("dbgl", n("dbgl"))])

    with nc.Block() as blk:
        plan.build(blk)
    return nc



def kernel(L0_attn_resid, L0_attn_Wq, L0_attn_Wk, L0_attn_Wv, L0_attn_Wo,
           L0_ffn_W_base, L0_ffn_b_base, L0_ffn_enc0, L0_ffn_enc1,
           L0_ffn_coefs0, L0_ffn_coefs1, L0_ffn_mask0, L0_ffn_mask1):
    x = np.ascontiguousarray(np.asarray(L0_attn_resid, dtype=np.float32)[0])
    Wq = np.asarray(L0_attn_Wq, dtype=np.float32)
    Wk = np.asarray(L0_attn_Wk, dtype=np.float32)
    Wv = np.asarray(L0_attn_Wv, dtype=np.float32)
    Wo = np.ascontiguousarray(np.asarray(L0_attn_Wo, dtype=np.float32))
    WbT = np.ascontiguousarray(np.asarray(L0_ffn_W_base, dtype=np.float32).T)
    bbv = np.ascontiguousarray(np.asarray(L0_ffn_b_base, np.float32)[None, :])
    enc01T = np.ascontiguousarray(
        np.concatenate([np.asarray(L0_ffn_enc0, np.float32).T,
                        np.asarray(L0_ffn_enc1, np.float32).T], axis=1))
    c0 = np.ascontiguousarray(
        np.asarray(L0_ffn_coefs0, np.float32).reshape(3, 32, 128, 32)
        .transpose(0, 2, 1, 3).reshape(3, 128, 1024))
    c1 = np.ascontiguousarray(
        np.asarray(L0_ffn_coefs1, np.float32).reshape(3, 32, 128, 8)
        .transpose(0, 2, 1, 3).reshape(3, 128, 256))
    m0 = np.ascontiguousarray(
        np.asarray(L0_ffn_mask0).astype(np.uint8).reshape(3, 32, 128, 32)
        .transpose(0, 2, 1, 3).reshape(3, 128, 1024))
    m1 = np.ascontiguousarray(
        np.asarray(L0_ffn_mask1).astype(np.uint8).reshape(3, 32, 128, 8)
        .transpose(0, 2, 1, 3).reshape(3, 128, 256))
    MT = build_MT().astype(BFNP)

    cosv, sinv = rope_tables()
    cos_t = np.empty((128, GT * 128), np.float32)
    sin_t = np.empty((128, GT * 128), np.float32)
    for g in range(GT):
        cb = cosv[g * 128:(g + 1) * 128]
        sb = sinv[g * 128:(g + 1) * 128]
        cos_t[:, g * 128:g * 128 + 64] = cb
        cos_t[:, g * 128 + 64:g * 128 + 128] = cb
        sin_t[:, g * 128:g * 128 + 64] = -sb
        sin_t[:, g * 128 + 64:g * 128 + 128] = sb

    triv = np.triu(np.ones((128, 128), np.float32)).astype(BFNP)
    onesv = np.ones((128, 128), np.float32).astype(BFNP)
    identv = np.eye(128, dtype=np.float32).astype(BFNP)

    nc = build_program()
    in_maps = []
    for c in range(NCORE):
        in_maps.append({
            "xc": np.ascontiguousarray(x[c * TPC:(c + 1) * TPC]),
            "wq_my": np.ascontiguousarray(Wq[:, c * 512:(c + 1) * 512]),
            "wkv_my": np.ascontiguousarray(
                np.concatenate([Wk[:, c * 128:(c + 1) * 128],
                                Wv[:, c * 128:(c + 1) * 128]], axis=1)),
            "wo": Wo, "wbT": WbT, "mt": MT, "enc01T": enc01T,
            "c0r": c0, "c1r": c1, "m0r": m0, "m1r": m1, "bb": bbv,
            "cosT": cos_t, "sinT": sin_t,
            "tri": triv, "onesc": onesv, "identb": identv,
            "cofs": np.array([[c * TPC, c]], np.int32),
        })
    res = run_bass_kernel_spmd(nc, in_maps, core_ids=list(range(NCORE)))
    pieces = [res.results[c]["out"] for c in range(NCORE)]
    return np.concatenate(pieces, axis=0)[None, :, :]


# version 05 (final)
# speedup vs baseline: 1.0305x; 1.0091x over previous
import math
import numpy as np
import ml_dtypes
import concourse.bass as bass
import concourse.mybir as mybir
from concourse.bass_utils import run_bass_kernel_spmd

F32 = mybir.dt.float32
BF16 = mybir.dt.bfloat16
U8 = mybir.dt.uint8
I32 = mybir.dt.int32
AF = mybir.ActivationFunctionType
ALU = mybir.AluOpType

BFNP = ml_dtypes.bfloat16

NCORE = 8
T = 8192
D = 4096
DH = 128
TPC = T // NCORE
NTT = TPC // 128
NDC = D // 128
NQH = 4
NTCH = T // 512
GT = T // 128
SCALE = 1.0 / math.sqrt(DH)

ENGINES = ("sync", "scalar", "vector", "tensor", "gpsimd")


def build_MT(n=D):
    j = np.arange(n, dtype=np.float64)[:, None]
    M = np.zeros((n, n), dtype=np.float64)
    M[:, 0] = 1.0 / math.sqrt(n)
    max_f = (n - 1) // 2
    f = np.arange(1, max_f + 1, dtype=np.float64)[None, :]
    ang = 2.0 * math.pi * f * j / n
    M[:, 1:1 + 2 * max_f:2] = math.sqrt(2.0 / n) * np.cos(ang)
    M[:, 2:2 + 2 * max_f:2] = math.sqrt(2.0 / n) * np.sin(ang)
    M[:, n - 1] = ((-1.0) ** j[:, 0]) / math.sqrt(n)
    return np.ascontiguousarray(M.T)


def rope_tables():
    theta, factor, lo, hi, L0 = 500000.0, 32.0, 1.0, 4.0, 8192
    idx = np.arange(0, DH, 2, dtype=np.float64) / DH
    freq = 1.0 / (2.0 * math.pi) * theta ** (-idx)
    freq_low, freq_high = lo / L0, hi / L0
    freq_scaled = np.where(freq < freq_low, freq / factor, freq)
    smooth = np.clip((L0 * freq - lo) / (hi - lo), 0.0, 1.0)
    freq_smooth = (1.0 - smooth) * (freq / factor) + smooth * freq
    is_mid = (freq >= freq_low) & (freq <= freq_high)
    freq = np.where(is_mid, freq_smooth, freq_scaled)
    pos = np.arange(T, dtype=np.float64)
    phase = 2.0 * math.pi * pos[:, None] * freq[None, :]
    return np.cos(phase), np.sin(phase)


class Plan:

    def __init__(self, nc):
        self.nc = nc
        self.items = {e: [] for e in ENGINES}
        self.handles = {}
        self.counts = {}

    def sem(self, name):
        if name not in self.handles:
            self.handles[name] = self.nc.alloc_semaphore(name)
            self.counts[name] = 0
        return name

    def n(self, name):
        return self.counts[name]

    def emit(self, eng, fn, waits=(), inc=None, inc_amt=1):
        if inc is not None:
            self.sem(inc)
            self.counts[inc] += inc_amt
        ws = [(self.sem(s), int(v)) for (s, v) in waits if v > 0]
        self.items[eng].append((fn, ws, inc, inc_amt))

    def build(self, blk):
        for engname in ("gpsimd", "scalar", "vector", "tensor", "sync"):
            items = self.items[engname]
            if not items:
                continue

            def make_body(items=items):
                def body(e):
                    for fn, ws, inc, amt in items:
                        for s, v in ws:
                            e.wait_ge(self.handles[s], v)
                        instr = fn(e)
                        if inc is not None:
                            instr.then_inc(self.handles[inc], amt)
                return body
            getattr(blk, engname)(make_body())


def build_program(dbg=(), upto="p6"):
    nc = bass.Bass("TRN2", target_bir_lowering=False, debug=False)

    xc = nc.dram_tensor("xc", [TPC, D], F32, kind="ExternalInput").ap()
    wq_my = nc.dram_tensor("wq_my", [D, 512], F32, kind="ExternalInput").ap()
    wkv_my = nc.dram_tensor("wkv_my", [D, 256], F32, kind="ExternalInput").ap()
    wo = nc.dram_tensor("wo", [D, D], F32, kind="ExternalInput").ap()
    wbT = nc.dram_tensor("wbT", [D, D], F32, kind="ExternalInput").ap()
    mt = nc.dram_tensor("mt", [D, D], BF16, kind="ExternalInput").ap()
    enc01T = nc.dram_tensor("enc01T", [D, 40], F32, kind="ExternalInput").ap()
    c0r = nc.dram_tensor("c0r", [3, 128, 1024], F32, kind="ExternalInput").ap()
    c1r = nc.dram_tensor("c1r", [3, 128, 256], F32, kind="ExternalInput").ap()
    m0r = nc.dram_tensor("m0r", [3, 128, 1024], U8, kind="ExternalInput").ap()
    m1r = nc.dram_tensor("m1r", [3, 128, 256], U8, kind="ExternalInput").ap()
    bbt = nc.dram_tensor("bb", [1, D], F32, kind="ExternalInput").ap()
    cosT = nc.dram_tensor("cosT", [128, GT * 128], F32, kind="ExternalInput").ap()
    sinT = nc.dram_tensor("sinT", [128, GT * 128], F32, kind="ExternalInput").ap()
    tri = nc.dram_tensor("tri", [128, 128], BF16, kind="ExternalInput").ap()
    onesc = nc.dram_tensor("onesc", [128, 128], BF16, kind="ExternalInput").ap()
    identb = nc.dram_tensor("identb", [128, 128], BF16, kind="ExternalInput").ap()
    cofs = nc.dram_tensor("cofs", [1, 2], I32, kind="ExternalInput").ap()

    out = nc.dram_tensor("out", [TPC, D], F32, kind="ExternalOutput").ap()

    cin_xn = nc.dram_tensor("cin_xn", [D, TPC], BF16, kind="Internal").ap()
    cout_xn = nc.dram_tensor("cout_xn", [NCORE * D, TPC], BF16, kind="Internal", addr_space="Shared").ap()
    cin_c = nc.dram_tensor("cin_c", [512, T], BF16, kind="Internal").ap()
    cout_c = nc.dram_tensor("cout_c", [NCORE * 512, T], BF16, kind="Internal", addr_space="Shared").ap()

    dbg_out = {}
    dbg_shapes = {"d_xn": [128, NDC * TPC], "d_q": [128, NQH * T],
                  "d_k": [128, T], "d_v": [128, T], "d_ctx": [512, T],
                  "d_e01": [64, TPC], "d_v01": [64, D], "d_w01": [128, 1280]}
    for name in dbg:
        dbg_out[name] = nc.dram_tensor(name, dbg_shapes[name], F32,
                                       kind="ExternalOutput").ap()

    base0 = (nc.sbuf_base + 31) // 32 * 32
    top = nc.sbuf_top
    off = [base0]

    def place(name, shape, dtype, at=None):
        nbytes = int(np.prod(shape[1:])) * dtype.itemsize if hasattr(dtype, "itemsize") else None
        size = shape[1] * {F32: 4, BF16: 2, U8: 1, I32: 4}[dtype]
        pos = off[0] if at is None else at
        t = nc.alloc_sbuf_tensor_at(name, shape, dtype, offset=pos)
        if at is None:
            off[0] = (pos + size + 31) // 32 * 32
        return t

    xn_fm = place("xn_fm", [128, NDC * TPC], BF16)
    regB = place("regB", [128, 32768], BF16)
    k_res = place("k_res", [128, T], BF16)
    v_res = place("v_res", [128, T], BF16)
    v01b = place("v01b", [128, D], BF16)
    e01b = place("e01b", [128, TPC], BF16)
    tri_sb = place("tri_sb", [128, 128], BF16)
    ones_sb = place("ones_sb", [128, 128], BF16)
    id_sb = place("id_sb", [128, 128], BF16)
    cofs_sb = place("cofs_sb", [1, 8], I32)
    R = off[0]
    assert top - R >= 36000, f"R too small: {top - R}"

    regB_at = regB.manual_sbuf_range[0]
    xnt = place("xnt", [128, D], BF16, at=regB_at)
    xbuf = [place(f"xbuf{i}", [128, D], F32, at=regB_at + 8192 + 16384 * i)
            for i in range(2)]
    bnbuf = place("bnbuf", [128, 48], F32, at=regB_at + 41216)
    stats = place("stats", [128, 8], F32, at=regB_at + 41472)
    mtb = [place(f"mtb{i}", [128, 512], BF16, at=R + 20640 + 1024 * i) for i in range(8)]
    cf_sb = place("cf_sb", [128, 1280], F32, at=R)
    mk_sb = place("mk_sb", [128, 1280], F32, at=R + 5152)
    enc_sb = place("enc_sb", [128, NDC * 40], BF16, at=R + 18048)
    w01_f = place("w01_f", [128, NDC * 40], F32, at=R + 10304)
    w01_b = place("w01_b", [128, NDC * 40], BF16, at=R + 15456)
    xnw = [place(f"xnw{i}", [128, 1024], BF16, at=R + 2048 * i) for i in range(3)]
    wqw = [place(f"wqw{i}", [128, 1024], BF16, at=R + 6144 + 2048 * i) for i in range(3)]
    wkw = [place(f"wkw{i}", [128, 512], BF16, at=R + 12288 + 1024 * i) for i in range(3)]
    scrq = place("scrq", [128, 512], F32, at=R + 15360)
    scrq2 = place("scrq2", [128, 512], F32, at=R + 17408)
    scrb = [place(f"scrb{i}", [128, 512], BF16, at=R + 19456 + 1024 * i) for i in range(2)]
    kscr = [place(f"kscr{i}", [128, 128], BF16, at=R + 21504 + 256 * i) for i in range(2)]
    ropc = [place(f"ropc{i}", [128, 512], F32, at=R + 22016 + 2048 * i) for i in range(2)]
    rops = [place(f"rops{i}", [128, 512], F32, at=R + 26112 + 2048 * i) for i in range(2)]
    pt_sb = [place(f"pt_sb{i}", [128, 512], BF16, at=R + 1024 * i) for i in range(4)]
    rsr_sb = [place(f"rsr{i}", [128, 512], F32, at=R + 4096 + 2048 * i) for i in range(2)]
    ctxb = [place(f"ctxb{i}", [128, 512], BF16, at=R + 8192 + 1024 * i) for i in range(2)]
    w6 = [place(f"w6_{i}", [128, 512], BF16, at=R + 12288 + 1024 * i) for i in range(8)]
    xld = [place(f"xld{i}", [128, 512], F32, at=R + 4096 + 2048 * i) for i in range(2)]
    outsb = [place(f"outsb{i}", [128, 512], F32, at=R + 8192 + 2048 * i) for i in range(2)]

    ps = [nc.alloc_psum_tensor(f"ps{i}", [128, 512], F32) for i in range(6)]
    psb_guard = [nc.psum_tensor(f"psb{i}", [128, 640], BF16) for i in range(2)]
    psb = [g.__enter__() for g in psb_guard]

    plan = Plan(nc)
    E, S, n = plan.emit, plan.sem, plan.n
    for s in (["s_ld", "xb", "cfl", "enl", "dbgl", "pe", "pe_s", "pe_pv",
               "ac", "scp", "dv", "dvm", "cc", "stc", "sto", "co_ld",
               "bb_ld", "cx_ld", "cst0", "cst1", "so0", "so1", "xl0", "xl1"]
              + [f"mtb{i}" for i in range(8)] + [f"xnw{i}" for i in range(3)]
              + [f"wv{i}" for i in range(3)] + [f"rop{i}" for i in range(2)]
              + [f"w6s{i}" for i in range(8)] + ["xb0", "xb1"]):
        S(s)

    E("sync", lambda e: e.dma_start(tri_sb[:], tri), inc="s_ld", inc_amt=16)
    E("sync", lambda e: e.dma_start(ones_sb[:], onesc), inc="s_ld", inc_amt=16)
    E("sync", lambda e: e.dma_start(id_sb[:], identb), inc="s_ld", inc_amt=16)
    E("sync", lambda e: e.dma_start(cofs_sb[:, 0:2], cofs), inc="co_ld", inc_amt=16)
    E("gpsimd", lambda e: e.dma_start(v01b[40:41, :], bbt), inc="bb_ld", inc_amt=16)

    xc_t = xc.rearrange("(t p) c -> t p c", p=128)
    xn_v = xn_fm.ap().rearrange("p (d t) -> p d t", t=TPC)
    p1_ts_dv = []
    p1_tr_pe = []
    p1_evac_scp = []
    for tt in range(NTT):
        par = tt % 2
        w = [("dv", p1_ts_dv[tt - 2])] if tt >= 2 else []
        E("sync", lambda e, tt=tt, par=par: e.dma_start(xbuf[par][:], xc_t[tt]),
          waits=w, inc=f"xb{par}", inc_amt=16)
        sld = (tt // 2 + 1) * 16
        for k in range(8):
            wk = [(f"xb{par}", sld)] if k == 0 else []
            E("vector", lambda e, k=k, par=par: e.bn_stats(
                bnbuf[:, 6 * k:6 * k + 6], xbuf[par][:, 512 * k:512 * k + 512]),
              waits=wk, inc="dv" if k == 7 else None)
        E("vector", lambda e: e.bn_aggr(stats[:, 0:2], bnbuf[:]),
          waits=[("dv", n("dv"))], inc="dv")
        E("vector", lambda e: e.tensor_single_scalar(
            stats[:, 4:5], stats[:, 1:2], 1e-5, ALU.add),
          waits=[("dv", n("dv"))], inc="dv")
        E("scalar", lambda e: e.activation(stats[:, 2:3], stats[:, 4:5],
                                           AF.Ln),
          waits=[("dv", n("dv"))], inc="scp")
        E("scalar", lambda e: e.activation(stats[:, 3:4], stats[:, 2:3],
                                           AF.Exp, scale=-0.5),
          waits=[("scp", n("scp"))], inc="scp")
        w = [("scp", n("scp"))]
        if tt >= 1:
            w.append(("pe", p1_tr_pe[tt - 1]))
        E("vector", lambda e, par=par: e.tensor_scalar(
            xnt[:], xbuf[par][:], stats[:, 0:1], stats[:, 3:4],
            ALU.subtract, ALU.mult), waits=w, inc="dv")
        p1_ts_dv.append(n("dv"))
        for g4 in range(8):
            pb = (tt * 8 + g4) % 2
            for j in range(4):
                dchunk = g4 * 4 + j
                w = []
                if j == 0:
                    w.append(("dv", p1_ts_dv[tt]))
                    if tt == 0 and g4 == 0:
                        w.append(("s_ld", 3 * 16))
                    if len(p1_evac_scp) >= 2:
                        w.append(("scp", p1_evac_scp[-2]))
                E("tensor", lambda e, pb=pb, j=j, dchunk=dchunk: e.transpose(
                    psb[pb][:, 128 * j:128 * j + 128],
                    xnt[:, 128 * dchunk:128 * dchunk + 128], id_sb[:]),
                  waits=w, inc="pe" if j == 3 else None)
            E("scalar", lambda e, pb=pb, g4=g4, tt=tt: e.copy(
                xn_v[:, g4 * 4:g4 * 4 + 4, tt * 128:tt * 128 + 128],
                psb[pb].ap()[:, 0:512].rearrange("p (j c) -> p j c", j=4)),
              waits=[("pe", n("pe"))], inc="scp")
            p1_evac_scp.append(n("scp"))
        p1_tr_pe.append(n("pe"))
    xn_done_scp = n("scp")

    cin_xn_t = cin_xn.rearrange("(d p) c -> d p c", p=128)
    for dchunk in range(NDC):
        E("scalar", lambda e, dchunk=dchunk: e.dma_start(
            cin_xn_t[dchunk], xn_fm[:, dchunk * TPC:(dchunk + 1) * TPC]),
          waits=[("scp", xn_done_scp)] if dchunk == 0 else [],
          inc="stc", inc_amt=16)
    E("gpsimd", lambda e: e.collective_compute(
        "AllGather", ALU.bypass, replica_groups=[list(range(NCORE))],
        ins=[cin_xn.opt()], outs=[cout_xn.opt()]),
      waits=[("stc", n("stc"))], inc="cc")

    if "d_xn" in dbg_out:
        for i in range(NDC * TPC // 512):
            E("gpsimd", lambda e, i=i: e.dma_start(
                dbg_out["d_xn"][:, i * 512:(i + 1) * 512],
                xn_fm[:, i * 512:(i + 1) * 512]),
              waits=[("scp", xn_done_scp)], inc="dbgl", inc_amt=16)

    if upto == "p1":
        with nc.Block() as blk:
            plan.build(blk)
        return nc

    p3_dv_acc = 0
    for ei in range(3):
        w = [] if ei == 0 else [("dv", p3_dv_acc)]
        E("gpsimd", lambda e, ei=ei: e.dma_start(cf_sb[:, 0:1024], c0r[ei]),
          waits=w, inc="cfl", inc_amt=16)
        E("gpsimd", lambda e, ei=ei: e.dma_start(cf_sb[:, 1024:1280], c1r[ei]),
          inc="cfl", inc_amt=16)
        E("gpsimd", lambda e, ei=ei: e.dma_start(mk_sb[:, 0:1024], m0r[ei]),
          inc="cfl", inc_amt=16)
        E("gpsimd", lambda e, ei=ei: e.dma_start(mk_sb[:, 1024:1280], m1r[ei]),
          inc="cfl", inc_amt=16)
        gl = n("cfl")
        if ei == 0:
            E("vector", lambda e: e.tensor_tensor(w01_f[:, 0:1280], cf_sb[:],
                                                  mk_sb[:], ALU.mult),
              waits=[("cfl", gl)], inc="dv")
        else:
            E("vector", lambda e: e.tensor_tensor(mk_sb[:], cf_sb[:], mk_sb[:],
                                                  ALU.mult),
              waits=[("cfl", gl)], inc="dv")
            E("vector", lambda e: e.tensor_tensor(w01_f[:, 0:1280],
                                                  w01_f[:, 0:1280], mk_sb[:],
                                                  ALU.add),
              waits=[("dv", n("dv"))], inc="dv")
        p3_dv_acc = n("dv")
    w01bv = w01_b.ap().rearrange("p (ch r) -> p ch r", r=40)
    E("vector", lambda e: e.tensor_copy(
        w01bv[:, :, 0:32],
        w01_f[:, 0:1024].rearrange("p (ch r) -> p ch r", r=32)),
      waits=[("dv", p3_dv_acc)], inc="dv")
    E("vector", lambda e: e.tensor_copy(
        w01bv[:, :, 32:40],
        w01_f[:, 1024:1280].rearrange("p (ch r) -> p ch r", r=8)),
      inc="dv")
    w01_ready = n("dv")
    E("gpsimd", lambda e: e.dma_start(enc_sb[:], enc01T.rearrange(
        "(ch p) r -> p ch r", p=128)), inc="enl", inc_amt=16)
    E("vector", lambda e: e.memset(e01b[32:64, :], 1.0), inc="dv")
    e01_ones_dv = n("dv")

    mt_r = mt.rearrange("(ch p) c -> ch p c", p=128)
    mt_grp = []
    v01_scp = []
    mload = 0
    mtb_cnt = [0] * 8
    for jb in range(8):
        for ch in range(NDC):
            ring = mload % 8
            w = []
            if mload >= 8:
                w.append(("pe", mt_grp[mload - 8]))
            E("gpsimd", lambda e, ch=ch, jb=jb, ring=ring: e.dma_start(
                mtb[ring][:], mt_r[ch][:, jb * 512:(jb + 1) * 512]),
              waits=w, inc=f"mtb{ring}", inc_amt=16)
            mload += 1
            mtb_cnt[ring] += 1
            w = [(f"mtb{ring}", mtb_cnt[ring] * 16)]
            if ch == 0:
                w.append(("dv", w01_ready))
                if jb >= 2:
                    w.append(("scp", v01_scp[jb - 2]))
            E("tensor", lambda e, ch=ch, jb=jb, ring=ring: e.matmul(
                ps[4 + jb % 2][0:40, :], w01_b[:, ch * 40:(ch + 1) * 40],
                mtb[ring][:], start=(ch == 0), stop=(ch == NDC - 1)),
              waits=w, inc="pe")
            mt_grp.append(n("pe"))
        E("scalar", lambda e, jb=jb: e.mul(
            v01b[0:40, jb * 512:(jb + 1) * 512], ps[4 + jb % 2][0:40, :], 0.5),
          waits=[("pe", n("pe"))], inc="scp")
        v01_scp.append(n("scp"))


    for tc in range(2):
        for ch in range(NDC):
            w = []
            if ch == 0:
                w = [("enl", 16), ("scp", v01_scp[tc + 6]),
                     ("scp", xn_done_scp)]
            E("tensor", lambda e, ch=ch, tc=tc: e.matmul(
                ps[4 + tc][0:40, :], enc_sb[:, ch * 40:(ch + 1) * 40],
                xn_fm[:, ch * TPC + tc * 512: ch * TPC + tc * 512 + 512],
                start=(ch == 0), stop=(ch == NDC - 1)),
              waits=w, inc="pe" if ch == NDC - 1 else None)
        E("scalar", lambda e, tc=tc: e.copy(
            e01b[0:40, tc * 512:(tc + 1) * 512], ps[4 + tc][0:40, :]),
          waits=[("pe", n("pe")), ("dv", e01_ones_dv)], inc="scp")
    p3b_done_scp = n("scp")
    p3b_done_pe = n("pe")

    if "d_w01" in dbg_out:
        E("gpsimd", lambda e: e.dma_start(dbg_out["d_w01"], w01_b[:]),
          waits=[("dv", w01_ready)], inc="dbgl", inc_amt=16)
    if "d_v01" in dbg_out:
        for i in range(8):
            E("gpsimd", lambda e, i=i: e.dma_start(
                dbg_out["d_v01"][0:64, i * 512:(i + 1) * 512],
                v01b[0:64, i * 512:(i + 1) * 512]),
              waits=[("scp", p3b_done_scp)], inc="dbgl", inc_amt=16)
    if "d_e01" in dbg_out:
        for i in range(2):
            E("gpsimd", lambda e, i=i: e.dma_start(
                dbg_out["d_e01"][0:64, i * 512:(i + 1) * 512],
                e01b[0:64, i * 512:(i + 1) * 512]),
              waits=[("scp", p3b_done_scp)], inc="dbgl", inc_amt=16)

    if upto == "p3b":
        with nc.Block() as blk:
            plan.build(blk)
        return nc

    cout_r = cout_xn.rearrange("(r d p) c -> r d p c", p=128, d=NDC)
    wq_r = wq_my.rearrange("(d p) c -> d p c", p=128)
    wkv_r = wkv_my.rearrange("(d p) c -> d p c", p=128)
    q_fm = regB.ap().rearrange("p (h t) -> p h t", h=NQH)

    grp_pe = []
    rop_cnt = [0, 0]
    xnw_cnt = [0] * 3
    wv_cnt = [0] * 3
    ld_i = 0
    q_dv = {}
    kv_dv = {}
    v_scp = {}
    evac_scp = []
    qtr_pe = {}
    ktr_pe = {}
    tab_tch_dv = []

    for tch in range(NTCH):
        r, sl = tch // 2, tch % 2
        tpar = tch % 2
        w = []
        if tch < 2:
            w.append(("pe", p3b_done_pe))
        else:
            w.append(("dv", tab_tch_dv[tch - 2]))
        E("sync", lambda e, tch=tch, tpar=tpar: e.dma_start(
            ropc[tpar][:], cosT[:, tch * 512:(tch + 1) * 512]),
          waits=w, inc=f"rop{tpar}", inc_amt=16)
        E("sync", lambda e, tch=tch, tpar=tpar: e.dma_start(
            rops[tpar][:], sinT[:, tch * 512:(tch + 1) * 512]),
          inc=f"rop{tpar}", inc_amt=16)
        rop_cnt[tpar] += 2
        tab_par, tab_here = tpar, rop_cnt[tpar] * 16

        for dp in range(16):
            ring = ld_i % 3
            w = []
            if ld_i == 0:
                w.append(("cc", 1))
            if ld_i < 3:
                w.append(("pe", p3b_done_pe))
            if ld_i >= 3:
                w.append(("pe", grp_pe[ld_i - 3]))
            E("sync", lambda e, r=r, sl=sl, dp=dp, ring=ring: e.dma_start(
                xnw[ring].ap().rearrange("p (d c) -> p d c", d=2),
                cout_r[r][dp * 2:dp * 2 + 2, :, sl * 512:(sl + 1) * 512]
                .rearrange("d p c -> p d c")),
              waits=w, inc=f"xnw{ring}", inc_amt=16)
            wg = [("pe", grp_pe[ld_i - 3] if ld_i >= 3 else p3b_done_pe)]
            E("gpsimd", lambda e, dp=dp, ring=ring: e.dma_start(
                wqw[ring].ap().rearrange("p (d c) -> p d c", d=2),
                wq_r[dp * 2:dp * 2 + 2, :, :].rearrange("d p c -> p d c")),
              waits=wg, inc=f"wv{ring}", inc_amt=16)
            E("gpsimd", lambda e, dp=dp, ring=ring: e.dma_start(
                wkw[ring].ap().rearrange("p (d c) -> p d c", d=2),
                wkv_r[dp * 2:dp * 2 + 2, :, :].rearrange("d p c -> p d c")),
              inc=f"wv{ring}", inc_amt=16)
            ld_i += 1
            xnw_cnt[ring] += 1
            wv_cnt[ring] += 2
            xw_here, w2_here = xnw_cnt[ring] * 16, wv_cnt[ring] * 16

            for dd in range(2):
                for tt in range(4):
                    gtile = tch * 4 + tt
                    w = []
                    if dd == 0 and tt == 0:
                        w = [(f"xnw{ring}", xw_here), (f"wv{ring}", w2_here)]
                    if dp == 0 and dd == 0 and (gtile - 4) in q_dv:
                        w.append(("dv", q_dv[gtile - 4]))
                    E("tensor", lambda e, dd=dd, tt=tt, ring=ring, dp=dp: e.matmul(
                        ps[tt][:], xnw[ring][:, dd * 512 + tt * 128: dd * 512 + tt * 128 + 128],
                        wqw[ring][:, dd * 512:(dd + 1) * 512],
                        start=(dp == 0 and dd == 0), stop=(dp == 15 and dd == 1),
                        skip_group_check=True), waits=w)
                for tt in range(4):
                    gtile = tch * 4 + tt
                    kvbk, kvc = 4 + tt // 2, (tt % 2) * 256
                    w = []
                    if dp == 0 and dd == 0 and tt % 2 == 0:
                        if tch == 0:
                            w.append(("scp", p3b_done_scp))
                        for gprev in (gtile - 4, gtile - 3):
                            if gprev in kv_dv:
                                w.append(("dv", kv_dv[gprev]))
                                w.append(("scp", v_scp[gprev]))
                    E("tensor", lambda e, dd=dd, tt=tt, ring=ring, kvbk=kvbk, kvc=kvc, dp=dp: e.matmul(
                        ps[kvbk][:, kvc:kvc + 256],
                        xnw[ring][:, dd * 512 + tt * 128: dd * 512 + tt * 128 + 128],
                        wkw[ring][:, dd * 256:(dd + 1) * 256],
                        start=(dp == 0 and dd == 0 and tt % 2 == 0),
                        stop=(dp == 15 and dd == 1),
                        skip_group_check=True),
                      waits=w, inc="pe" if (dd == 1 and tt == 3) else None)
            grp_pe.append(n("pe"))
        tch_mm_pe = n("pe")

        for tt in range(4):
            gtile = tch * 4 + tt
            spar = gtile % 2
            kvbk, kvc = 4 + tt // 2, (tt % 2) * 256
            tcol = tt * 128
            cosb = ropc[tpar].ap().rearrange(
                "p (a f) -> p a f", a=4)[:, tt:tt + 1, :].broadcast_to((128, 4, 128))
            sinb = rops[tpar].ap().rearrange(
                "p (a f) -> p a f", a=4)[:, tt:tt + 1, :].broadcast_to((128, 4, 128))
            cos2 = ropc[tpar].ap().rearrange(
                "p (a h f) -> p a h f", a=4, h=2)[:, tt:tt + 1, :, :].broadcast_to((128, 4, 2, 64))
            sin2 = rops[tpar].ap().rearrange(
                "p (a h f) -> p a h f", a=4, h=2)[:, tt:tt + 1, :, :].broadcast_to((128, 4, 2, 64))
            qv = ps[tt].ap().rearrange("p (b h f) -> p b h f", b=4, h=2)
            s2v = scrq2.ap().rearrange("p (b h f) -> p b h f", b=4, h=2)
            w = [("pe", tch_mm_pe), (f"rop{tab_par}", tab_here)]
            E("vector", lambda e, qv=qv, s2v=s2v, sin2=sin2: e.tensor_tensor(
                s2v[:, :, 0, :], qv[:, :, 1, :], sin2[:, :, 0, :], ALU.mult),
              waits=w, inc="dv")
            E("vector", lambda e, qv=qv, s2v=s2v, sin2=sin2: e.tensor_tensor(
                s2v[:, :, 1, :], qv[:, :, 0, :], sin2[:, :, 1, :], ALU.mult),
              inc="dv")
            E("vector", lambda e, tt=tt, cosb=cosb: e.tensor_tensor(
                scrq.ap().rearrange("p (b f) -> p b f", b=4),
                ps[tt].ap().rearrange("p (b f) -> p b f", b=4), cosb, ALU.mult),
              inc="dv")
            q_dv[gtile] = n("dv")
            w = [("dv", n("dv"))]
            if spar in qtr_pe:
                w.append(("pe", qtr_pe[spar]))
            E("vector", lambda e, spar=spar: e.tensor_tensor(
                scrb[spar][:], scrq[:], scrq2[:], ALU.add), waits=w, inc="dv")
            scrb_dv = n("dv")
            kv2 = ps[kvbk].ap()[:, kvc:kvc + 128].rearrange(
                "p (h f) -> p h f", h=2)
            s2k = scrq2.ap()[:, 0:128].rearrange("p (h f) -> p h f", h=2)
            c1b = ropc[tpar].ap().rearrange(
                "p (a h f) -> p a h f", a=4, h=2)[:, tt, :, :]
            s1b = rops[tpar].ap().rearrange(
                "p (a h f) -> p a h f", a=4, h=2)[:, tt, :, :]
            E("vector", lambda e, kv2=kv2, s2k=s2k, s1b=s1b: e.tensor_tensor(
                s2k[:, 0, :], kv2[:, 1, :], s1b[:, 0, :], ALU.mult),
              waits=[("dv", scrb_dv)], inc="dv")
            E("vector", lambda e, kv2=kv2, s2k=s2k, s1b=s1b: e.tensor_tensor(
                s2k[:, 1, :], kv2[:, 0, :], s1b[:, 1, :], ALU.mult), inc="dv")
            E("vector", lambda e, kvbk=kvbk, kvc=kvc, c1b=c1b: e.tensor_tensor(
                scrq.ap()[:, 0:128].rearrange("p (h f) -> p h f", h=2),
                ps[kvbk].ap()[:, kvc:kvc + 128].rearrange("p (h f) -> p h f", h=2),
                c1b, ALU.mult), inc="dv")
            kv_dv[gtile] = n("dv")
            if tt == 3:
                tab_tch_dv.append(n("dv"))
            w = [("dv", n("dv"))]
            if spar in ktr_pe:
                w.append(("pe", ktr_pe[spar]))
            E("vector", lambda e, spar=spar: e.tensor_tensor(
                kscr[spar][:], scrq[:, 0:128], scrq2[:, 0:128], ALU.add),
              waits=w, inc="dv")
            kscr_dv = n("dv")
            E("scalar", lambda e, kvbk=kvbk, kvc=kvc, gtile=gtile: e.copy(
                v_res[:, gtile * 128:(gtile + 1) * 128],
                ps[kvbk][:, kvc + 128:kvc + 256]),
              waits=[("pe", tch_mm_pe)], inc="scp")
            v_scp[gtile] = n("scp")
            for h in range(NQH):
                w = []
                if h == 0:
                    w = [("dv", scrb_dv)]
                    if len(evac_scp) >= 2:
                        w.append(("scp", evac_scp[-2]))
                E("tensor", lambda e, h=h, spar=spar: e.transpose(
                    psb[spar][:, h * 128:(h + 1) * 128],
                    scrb[spar][:, h * 128:(h + 1) * 128], id_sb[:]),
                  waits=w, inc="pe" if h == 3 else None)
            qtr_pe[spar] = n("pe")
            E("tensor", lambda e, spar=spar: e.transpose(
                psb[spar][:, 512:640], kscr[spar][:], id_sb[:]),
              waits=[("dv", kscr_dv)], inc="pe")
            ktr_pe[spar] = n("pe")
            E("scalar", lambda e, spar=spar, gtile=gtile: e.copy(
                q_fm[:, :, gtile * 128:(gtile + 1) * 128],
                psb[spar].ap()[:, 0:512].rearrange("p (h c) -> p h c", h=4)),
              waits=[("pe", n("pe"))], inc="scp")
            E("scalar", lambda e, spar=spar, gtile=gtile: e.copy(
                k_res[:, gtile * 128:(gtile + 1) * 128], psb[spar][:, 512:640]),
              inc="scp")
            evac_scp.append(n("scp"))
    p2_done_scp = n("scp")

    for nm, buf, wd in (("d_q", regB, NQH * T), ("d_k", k_res, T), ("d_v", v_res, T)):
        if nm in dbg_out:
            for i in range(wd // 512):
                E("gpsimd", lambda e, i=i, nm=nm, buf=buf: e.dma_start(
                    dbg_out[nm][:, i * 512:(i + 1) * 512],
                    buf[:, i * 512:(i + 1) * 512]),
                  waits=[("scp", p2_done_scp)], inc="dbgl", inc_amt=16)

    if upto == "p2":
        with nc.Block() as blk:
            plan.build(blk)
        return nc

    for g in reversed(psb_guard):
        g.__exit__(None, None, None)
    ps6 = [nc.alloc_psum_tensor(f"ps6_{i}", [128, 512], F32) for i in range(2)]
    p6b = ps + ps6
    units = []
    gid = 0
    for hq in range(NQH):
        for qch in range(NTCH):
            nkt = (qch + 1) * 4
            for kt in range(nkt):
                diag = kt >= qch * 4
                qoff = (kt - qch * 4) * 128 if diag else 0
                units.append((hq, qch, kt, diag, qoff, 512 - qoff,
                              hq * T + qch * 512 + qoff, gid, kt == nkt - 1))
            gid += 1
    NU = len(units)
    ctx_of = lambda g: (ps6[0] if g % 2 else ps[4])
    rs_of = lambda g: (ps6[1] if g % 2 else ps[5])

    mask_cnt = [0]
    mask_idx = {}
    ev_dv = []
    grp_pv = {}
    cst_cnt = [0, 0]

    def emit_S(u):
        hq, qch, kt, diag, qoff, N, qcol, g, last = units[u]
        w = []
        if u == 0:
            w.append(("scp", p2_done_scp))
        if u >= 4:
            w.append(("ac", u - 3))
        E("tensor", lambda e, kt=kt, qcol=qcol, qoff=qoff, N=N, sbk=u % 4: e.matmul(
            ps[sbk][:, qoff:qoff + N], k_res[:, kt * 128:(kt + 1) * 128],
            regB[:, qcol:qcol + N], start=True, stop=True),
          waits=w, inc="pe_s")

    def emit_exp(u):
        hq, qch, kt, diag, qoff, N, qcol, g, last = units[u]
        w = [("pe_s", u + 1)]
        if u >= 4:
            w.append(("pe_pv", u - 3))
        E("scalar", lambda e, sbk=u % 4, qoff=qoff, N=N, u=u: e.activation(
            pt_sb[u % 4][:, qoff:qoff + N], ps[sbk][:, qoff:qoff + N],
            AF.Exp, scale=SCALE),
          waits=w, inc="ac")
        if diag:
            E("vector", lambda e, u=u, qoff=qoff: e.tensor_tensor(
                pt_sb[u % 4][:, qoff:qoff + 128],
                pt_sb[u % 4][:, qoff:qoff + 128], tri_sb[:], ALU.mult),
              waits=[("ac", u + 1)], inc="dvm")
            mask_cnt[0] += 1
            mask_idx[u] = mask_cnt[0]

    def emit_rspv(u):
        hq, qch, kt, diag, qoff, N, qcol, g, last = units[u]
        w = [("ac", u + 1)]
        if diag:
            w.append(("dvm", mask_idx[u]))
        if kt == 0 and g >= 2:
            w.append(("dv", ev_dv[g - 2]))
        E("tensor", lambda e, u=u, qoff=qoff, N=N, kt=kt, g=g, last=last: e.matmul(
            rs_of(g)[:, qoff:qoff + N], ones_sb[:],
            pt_sb[u % 4][:, qoff:qoff + N],
            start=(kt == 0), stop=last, skip_group_check=True), waits=w)
        E("tensor", lambda e, u=u, qoff=qoff, N=N, kt=kt, g=g, last=last: e.matmul(
            ctx_of(g)[:, qoff:qoff + N], v_res[:, kt * 128:(kt + 1) * 128],
            pt_sb[u % 4][:, qoff:qoff + N],
            start=(kt == 0), stop=last, skip_group_check=True), inc="pe_pv")
        if last:
            grp_pv[g] = n("pe_pv")
            emit_evac(g, hq, qch)

    def emit_evac(g, hq, qch):
        epar = g % 2
        w = [("pe_pv", grp_pv[g])]
        if g >= 2:
            w.append(("dv", ev_dv[g - 2]))
        E("vector", lambda e, epar=epar, g=g: e.reciprocal(
            rsr_sb[epar][:], rs_of(g)[:]), waits=w, inc="dv")
        w = [("dv", n("dv"))]
        if g >= 2:
            w.append((f"cst{epar}", cst_cnt[epar] * 16))
        E("vector", lambda e, epar=epar, g=g: e.tensor_tensor(
            ctxb[epar][:], ctx_of(g)[:], rsr_sb[epar][:], ALU.mult),
          waits=w, inc="dv")
        ev_dv.append(n("dv"))
        E("scalar", lambda e, epar=epar, hq=hq, qch=qch: e.dma_start(
            cin_c[hq * 128:(hq + 1) * 128, qch * 512:(qch + 1) * 512],
            ctxb[epar][:]),
          waits=[("dv", n("dv"))], inc=f"cst{epar}", inc_amt=16)
        cst_cnt[epar] += 1

    for u in range(NU):
        emit_S(u)
        emit_exp(u)
        if u >= 2:
            emit_rspv(u - 2)
    emit_rspv(NU - 2)
    emit_rspv(NU - 1)
    p4_ac = n("ac")
    p4_pv = n("pe_pv")
    p4_dv_last = ev_dv[-1]

    wb_r = wbT.rearrange("(d p) c -> d p c", p=128)
    w6_cnt = [0] * 8
    for d in range(8):
        E("gpsimd", lambda e, d=d: e.dma_start(
            w6[d][:], wb_r[d][:, 0:512]),
          waits=[("scp", p2_done_scp)] if d == 0 else [],
          inc=f"w6s{d}", inc_amt=16)
        w6_cnt[d] += 1

    E("gpsimd", lambda e: e.collective_compute(
        "AllGather", ALU.bypass, replica_groups=[list(range(NCORE))],
        ins=[cin_c.opt()], outs=[cout_c.opt()]),
      waits=[("cst0", cst_cnt[0] * 16), ("cst1", cst_cnt[1] * 16)], inc="cc")

    if "d_ctx" in dbg_out:
        for i in range(4):
            E("gpsimd", lambda e, i=i: e.dma_start(
                dbg_out["d_ctx"][i * 128:(i + 1) * 128, :],
                cin_c[i * 128:(i + 1) * 128, :]),
              waits=[("cc", 2)], inc="dbgl", inc_amt=16)

    if upto == "p4":
        with nc.Block() as blk:
            plan.build(blk)
        return nc


    cout_c_p = cout_c.rearrange("(d p) c -> p d c", p=128)

    def load_ctx_branched(e):
        last = {}
        with e.register("cid") as cid:
            e.reg_load(cid, cofs_sb[0:1, 1:2])
            for k in range(NCORE):
                with e.If_eq(cid, k):
                    e.dma_start(
                        regB.ap().rearrange("p (d t) -> p d t", t=TPC),
                        cout_c_p[:, :, k * TPC:(k + 1) * TPC]
                    ).then_inc(plan.handles["cx_ld"], 16)
            e.end_ifs()
        return None

    S("cx_ld")
    plan.counts["cx_ld"] += 16
    plan.items["sync"].append((load_ctx_branched,
                               [(S("cc"), 2), (S("co_ld"), 16)], None, 1))
    ctx_ld = n("cx_ld")

    wo_r = wo.rearrange("(d p) c -> d p c", p=128)
    outv = out.rearrange("(t p) c -> t p c", p=128)

    wgrp_pe = []
    w6i = 0
    p6_ev_dv = []
    nst = 0
    xl_cnt = [0, 0]
    so_cnt = [0, 0]
    for ob in range(8):
        for si, (wsrc, lhs) in enumerate(((wb_r, xn_fm), (wo_r, regB))):
            for d in range(NDC):
                ring = w6i % 8
                if w6i >= 8:
                    w = []
                    if w6i >= 8:
                        w.append(("pe", wgrp_pe[w6i - 8]))
                    E("gpsimd", lambda e, d=d, ob=ob, ring=ring, wsrc=wsrc: e.dma_start(
                        w6[ring][:], wsrc[d][:, ob * 512:(ob + 1) * 512]),
                      waits=w, inc=f"w6s{ring}", inc_amt=16)
                    w6_cnt[ring] += 1
                w6i += 1
                wld = w6_cnt[ring] * 16
                for tt in range(NTT):
                    w = []
                    if tt == 0:
                        w = [(f"w6s{ring}", wld)]
                    if si == 0 and d == 0:
                        if ob == 0:
                            if tt == 0:
                                w.append(("ac", p4_ac))
                                w.append(("scp", p3b_done_scp))
                                w.append(("bb_ld", 16))
                            if tt == 4:
                                w.append(("dv", p4_dv_last))
                        else:
                            w.append(("dv", p6_ev_dv[(ob - 1) * 8 + tt]))
                    if si == 1 and d == 0 and tt == 0 and ob == 0:
                        w.append(("cx_ld", ctx_ld))
                    first = (si == 0 and d == 0)
                    E("tensor", lambda e, d=d, tt=tt, ring=ring, lhs=lhs, first=first: e.matmul(
                        p6b[tt][:],
                        lhs[:, d * TPC + tt * 128: d * TPC + tt * 128 + 128],
                        w6[ring][:], start=first, stop=False,
                        skip_group_check=True),
                      waits=w, inc="pe" if tt == NTT - 1 else None)
                wgrp_pe.append(n("pe"))
        for tt in range(NTT):
            E("tensor", lambda e, tt=tt, ob=ob: e.matmul(
                p6b[tt][:], e01b[0:41, tt * 128:(tt + 1) * 128],
                v01b[0:41, ob * 512:(ob + 1) * 512],
                start=False, stop=True, skip_group_check=True),
              inc="pe_pv")
            pv_here = n("pe_pv")
            xpar = (ob * NTT + tt) % 2
            w = []
            if len(p6_ev_dv) >= 2:
                w.append(("dv", p6_ev_dv[-2]))
            elif ob == 0 and tt < 2:
                w.append(("dv", p4_dv_last))
            E("sync", lambda e, ob=ob, tt=tt, xpar=xpar: e.dma_start(
                xld[xpar][:], xc_t[tt][:, ob * 512:(ob + 1) * 512]),
              waits=w, inc=f"xl{xpar}", inc_amt=16)
            xl_cnt[xpar] += 1
            w = [("pe_pv", pv_here), (f"xl{xpar}", xl_cnt[xpar] * 16)]
            if nst >= 2:
                w.append((f"so{nst % 2}", so_cnt[nst % 2] * 16))
            elif nst < 2:
                w.append((f"cst{nst % 2}", cst_cnt[nst % 2] * 16))
            E("vector", lambda e, tt=tt, xpar=xpar, opar=nst % 2: e.tensor_tensor(
                outsb[opar][:], p6b[tt][:], xld[xpar][:], ALU.add),
              waits=w, inc="dv")
            p6_ev_dv.append(n("dv"))
            E("scalar", lambda e, tt=tt, ob=ob, opar=nst % 2: e.dma_start(
                outv[tt][:, ob * 512:(ob + 1) * 512], outsb[opar][:]),
              waits=[("dv", n("dv"))], inc=f"so{nst % 2}", inc_amt=16)
            so_cnt[nst % 2] += 1
            nst += 1

    E("sync", lambda e: None, waits=[("so0", so_cnt[0] * 16),
                                     ("so1", so_cnt[1] * 16)])
    E("gpsimd", lambda e: None, waits=[("dbgl", n("dbgl"))])

    with nc.Block() as blk:
        plan.build(blk)
    return nc



def kernel(L0_attn_resid, L0_attn_Wq, L0_attn_Wk, L0_attn_Wv, L0_attn_Wo,
           L0_ffn_W_base, L0_ffn_b_base, L0_ffn_enc0, L0_ffn_enc1,
           L0_ffn_coefs0, L0_ffn_coefs1, L0_ffn_mask0, L0_ffn_mask1):
    x = np.ascontiguousarray(np.asarray(L0_attn_resid, dtype=np.float32)[0])
    Wq = np.asarray(L0_attn_Wq, dtype=np.float32)
    Wk = np.asarray(L0_attn_Wk, dtype=np.float32)
    Wv = np.asarray(L0_attn_Wv, dtype=np.float32)
    Wo = np.ascontiguousarray(np.asarray(L0_attn_Wo, dtype=np.float32))
    WbT = np.ascontiguousarray(np.asarray(L0_ffn_W_base, dtype=np.float32).T)
    bbv = np.ascontiguousarray(np.asarray(L0_ffn_b_base, np.float32)[None, :])
    enc01T = np.ascontiguousarray(
        np.concatenate([np.asarray(L0_ffn_enc0, np.float32).T,
                        np.asarray(L0_ffn_enc1, np.float32).T], axis=1))
    c0 = np.ascontiguousarray(
        np.asarray(L0_ffn_coefs0, np.float32).reshape(3, 32, 128, 32)
        .transpose(0, 2, 1, 3).reshape(3, 128, 1024))
    c1 = np.ascontiguousarray(
        np.asarray(L0_ffn_coefs1, np.float32).reshape(3, 32, 128, 8)
        .transpose(0, 2, 1, 3).reshape(3, 128, 256))
    m0 = np.ascontiguousarray(
        np.asarray(L0_ffn_mask0).astype(np.uint8).reshape(3, 32, 128, 32)
        .transpose(0, 2, 1, 3).reshape(3, 128, 1024))
    m1 = np.ascontiguousarray(
        np.asarray(L0_ffn_mask1).astype(np.uint8).reshape(3, 32, 128, 8)
        .transpose(0, 2, 1, 3).reshape(3, 128, 256))
    MT = build_MT().astype(BFNP)

    cosv, sinv = rope_tables()
    cos_t = np.empty((128, GT * 128), np.float32)
    sin_t = np.empty((128, GT * 128), np.float32)
    for g in range(GT):
        cb = cosv[g * 128:(g + 1) * 128]
        sb = sinv[g * 128:(g + 1) * 128]
        cos_t[:, g * 128:g * 128 + 64] = cb
        cos_t[:, g * 128 + 64:g * 128 + 128] = cb
        sin_t[:, g * 128:g * 128 + 64] = -sb
        sin_t[:, g * 128 + 64:g * 128 + 128] = sb

    triv = np.triu(np.ones((128, 128), np.float32)).astype(BFNP)
    onesv = np.ones((128, 128), np.float32).astype(BFNP)
    identv = np.eye(128, dtype=np.float32).astype(BFNP)

    nc = build_program()
    in_maps = []
    for c in range(NCORE):
        in_maps.append({
            "xc": np.ascontiguousarray(x[c * TPC:(c + 1) * TPC]),
            "wq_my": np.ascontiguousarray(Wq[:, c * 512:(c + 1) * 512]),
            "wkv_my": np.ascontiguousarray(
                np.concatenate([Wk[:, c * 128:(c + 1) * 128],
                                Wv[:, c * 128:(c + 1) * 128]], axis=1)),
            "wo": Wo, "wbT": WbT, "mt": MT, "enc01T": enc01T,
            "c0r": c0, "c1r": c1, "m0r": m0, "m1r": m1, "bb": bbv,
            "cosT": cos_t, "sinT": sin_t,
            "tri": triv, "onesc": onesv, "identb": identv,
            "cofs": np.array([[c * TPC, c]], np.int32),
        })
    res = run_bass_kernel_spmd(nc, in_maps, core_ids=list(range(NCORE)))
    pieces = [res.results[c]["out"] for c in range(NCORE)]
    return np.concatenate(pieces, axis=0)[None, :, :]


# version 06
# speedup vs baseline: 1.0354x; 1.0048x over previous
import math
import numpy as np
import ml_dtypes
import concourse.bass as bass
import concourse.mybir as mybir
from concourse.bass_utils import run_bass_kernel_spmd

F32 = mybir.dt.float32
BF16 = mybir.dt.bfloat16
U8 = mybir.dt.uint8
I32 = mybir.dt.int32
AF = mybir.ActivationFunctionType
ALU = mybir.AluOpType

BFNP = ml_dtypes.bfloat16

NCORE = 8
T = 8192
D = 4096
DH = 128
TPC = T // NCORE
NTT = TPC // 128
NDC = D // 128
NQH = 4
NTCH = T // 512
GT = T // 128
SCALE = 1.0 / math.sqrt(DH)

ENGINES = ("sync", "scalar", "vector", "tensor", "gpsimd")


def build_MT(n=D):
    j = np.arange(n, dtype=np.float64)[:, None]
    M = np.zeros((n, n), dtype=np.float64)
    M[:, 0] = 1.0 / math.sqrt(n)
    max_f = (n - 1) // 2
    f = np.arange(1, max_f + 1, dtype=np.float64)[None, :]
    ang = 2.0 * math.pi * f * j / n
    M[:, 1:1 + 2 * max_f:2] = math.sqrt(2.0 / n) * np.cos(ang)
    M[:, 2:2 + 2 * max_f:2] = math.sqrt(2.0 / n) * np.sin(ang)
    M[:, n - 1] = ((-1.0) ** j[:, 0]) / math.sqrt(n)
    return np.ascontiguousarray(M.T)


def rope_tables():
    theta, factor, lo, hi, L0 = 500000.0, 32.0, 1.0, 4.0, 8192
    idx = np.arange(0, DH, 2, dtype=np.float64) / DH
    freq = 1.0 / (2.0 * math.pi) * theta ** (-idx)
    freq_low, freq_high = lo / L0, hi / L0
    freq_scaled = np.where(freq < freq_low, freq / factor, freq)
    smooth = np.clip((L0 * freq - lo) / (hi - lo), 0.0, 1.0)
    freq_smooth = (1.0 - smooth) * (freq / factor) + smooth * freq
    is_mid = (freq >= freq_low) & (freq <= freq_high)
    freq = np.where(is_mid, freq_smooth, freq_scaled)
    pos = np.arange(T, dtype=np.float64)
    phase = 2.0 * math.pi * pos[:, None] * freq[None, :]
    return np.cos(phase), np.sin(phase)


class Plan:

    def __init__(self, nc):
        self.nc = nc
        self.items = {e: [] for e in ENGINES}
        self.handles = {}
        self.counts = {}

    def sem(self, name):
        if name not in self.handles:
            self.handles[name] = self.nc.alloc_semaphore(name)
            self.counts[name] = 0
        return name

    def n(self, name):
        return self.counts[name]

    def emit(self, eng, fn, waits=(), inc=None, inc_amt=1):
        if inc is not None:
            self.sem(inc)
            self.counts[inc] += inc_amt
        ws = [(self.sem(s), int(v)) for (s, v) in waits if v > 0]
        self.items[eng].append((fn, ws, inc, inc_amt))

    def build(self, blk):
        for engname in ("gpsimd", "scalar", "vector", "tensor", "sync"):
            items = self.items[engname]
            if not items:
                continue

            def make_body(items=items):
                def body(e):
                    for fn, ws, inc, amt in items:
                        for s, v in ws:
                            e.wait_ge(self.handles[s], v)
                        instr = fn(e)
                        if inc is not None:
                            instr.then_inc(self.handles[inc], amt)
                return body
            getattr(blk, engname)(make_body())


def build_program(dbg=(), upto="p6"):
    nc = bass.Bass("TRN2", target_bir_lowering=False, debug=False)

    xc = nc.dram_tensor("xc", [TPC, D], F32, kind="ExternalInput").ap()
    wq_my = nc.dram_tensor("wq_my", [D, 512], F32, kind="ExternalInput").ap()
    wkv_my = nc.dram_tensor("wkv_my", [D, 256], F32, kind="ExternalInput").ap()
    wo = nc.dram_tensor("wo", [D, D], F32, kind="ExternalInput").ap()
    wbT = nc.dram_tensor("wbT", [D, D], F32, kind="ExternalInput").ap()
    mt = nc.dram_tensor("mt", [D, D], BF16, kind="ExternalInput").ap()
    enc01T = nc.dram_tensor("enc01T", [D, 40], F32, kind="ExternalInput").ap()
    c0r = nc.dram_tensor("c0r", [3, 128, 1024], F32, kind="ExternalInput").ap()
    c1r = nc.dram_tensor("c1r", [3, 128, 256], F32, kind="ExternalInput").ap()
    m0r = nc.dram_tensor("m0r", [3, 128, 1024], U8, kind="ExternalInput").ap()
    m1r = nc.dram_tensor("m1r", [3, 128, 256], U8, kind="ExternalInput").ap()
    bbt = nc.dram_tensor("bb", [1, D], F32, kind="ExternalInput").ap()
    cosT = nc.dram_tensor("cosT", [128, GT * 128], F32, kind="ExternalInput").ap()
    sinT = nc.dram_tensor("sinT", [128, GT * 128], F32, kind="ExternalInput").ap()
    tri = nc.dram_tensor("tri", [128, 128], BF16, kind="ExternalInput").ap()
    onesc = nc.dram_tensor("onesc", [128, 128], BF16, kind="ExternalInput").ap()
    identb = nc.dram_tensor("identb", [128, 128], BF16, kind="ExternalInput").ap()
    cofs = nc.dram_tensor("cofs", [1, 2], I32, kind="ExternalInput").ap()

    out = nc.dram_tensor("out", [TPC, D], F32, kind="ExternalOutput").ap()

    cin_xn = nc.dram_tensor("cin_xn", [D, TPC], BF16, kind="Internal").ap()
    cout_xn = nc.dram_tensor("cout_xn", [NCORE * D, TPC], BF16, kind="Internal", addr_space="Shared").ap()
    cin_c = nc.dram_tensor("cin_c", [512, T], BF16, kind="Internal").ap()
    cout_c = nc.dram_tensor("cout_c", [NCORE * 512, T], BF16, kind="Internal", addr_space="Shared").ap()

    dbg_out = {}
    dbg_shapes = {"d_xn": [128, NDC * TPC], "d_q": [128, NQH * T],
                  "d_k": [128, T], "d_v": [128, T], "d_ctx": [512, T],
                  "d_e01": [64, TPC], "d_v01": [64, D], "d_w01": [128, 1280]}
    for name in dbg:
        dbg_out[name] = nc.dram_tensor(name, dbg_shapes[name], F32,
                                       kind="ExternalOutput").ap()

    base0 = (nc.sbuf_base + 31) // 32 * 32
    top = nc.sbuf_top
    off = [base0]

    def place(name, shape, dtype, at=None):
        nbytes = int(np.prod(shape[1:])) * dtype.itemsize if hasattr(dtype, "itemsize") else None
        size = shape[1] * {F32: 4, BF16: 2, U8: 1, I32: 4}[dtype]
        pos = off[0] if at is None else at
        t = nc.alloc_sbuf_tensor_at(name, shape, dtype, offset=pos)
        if at is None:
            off[0] = (pos + size + 31) // 32 * 32
        return t

    xn_fm = place("xn_fm", [128, NDC * TPC], BF16)
    regB = place("regB", [128, 32768], BF16)
    k_res = place("k_res", [128, T], BF16)
    v_res = place("v_res", [128, T], BF16)
    v01b = place("v01b", [128, D], BF16)
    e01b = place("e01b", [128, TPC], BF16)
    tri_sb = place("tri_sb", [128, 128], BF16)
    ones_sb = place("ones_sb", [128, 128], BF16)
    id_sb = place("id_sb", [128, 128], BF16)
    cofs_sb = place("cofs_sb", [1, 8], I32)
    R = off[0]
    assert top - R >= 36000, f"R too small: {top - R}"

    regB_at = regB.manual_sbuf_range[0]
    xnt = place("xnt", [128, D], BF16, at=regB_at)
    xbuf = [place(f"xbuf{i}", [128, D], F32, at=regB_at + 8192 + 16384 * i)
            for i in range(2)]
    bnbuf = place("bnbuf", [128, 48], F32, at=regB_at + 41216)
    stats = place("stats", [128, 8], F32, at=regB_at + 41472)
    mtb = [place(f"mtb{i}", [128, 512], BF16, at=R + 20640 + 1024 * i) for i in range(8)]
    cf_sb = place("cf_sb", [128, 1280], F32, at=R)
    mk_sb = place("mk_sb", [128, 1280], F32, at=R + 5152)
    enc_sb = place("enc_sb", [128, NDC * 40], BF16, at=R + 18048)
    w01_f = place("w01_f", [128, NDC * 40], F32, at=R + 10304)
    w01_b = place("w01_b", [128, NDC * 40], BF16, at=R + 15456)
    xnw = [place(f"xnw{i}", [128, 1024], BF16, at=R + 2048 * i) for i in range(3)]
    wqw = [place(f"wqw{i}", [128, 1024], BF16, at=R + 6144 + 2048 * i) for i in range(3)]
    wkw = [place(f"wkw{i}", [128, 512], BF16, at=R + 12288 + 1024 * i) for i in range(3)]
    scrq = place("scrq", [128, 512], F32, at=R + 15360)
    scrq2 = place("scrq2", [128, 512], F32, at=R + 17408)
    scrb = [place(f"scrb{i}", [128, 512], BF16, at=R + 19456 + 1024 * i) for i in range(2)]
    kscr = [place(f"kscr{i}", [128, 128], BF16, at=R + 21504 + 256 * i) for i in range(2)]
    ropc = [place(f"ropc{i}", [128, 512], F32, at=R + 22016 + 2048 * i) for i in range(2)]
    rops = [place(f"rops{i}", [128, 512], F32, at=R + 26112 + 2048 * i) for i in range(2)]
    pt_sb = [place(f"pt_sb{i}", [128, 512], BF16, at=R + 1024 * i) for i in range(4)]
    rsr_sb = [place(f"rsr{i}", [128, 512], F32, at=R + 4096 + 2048 * i) for i in range(2)]
    ctxb = [place(f"ctxb{i}", [128, 512], BF16, at=R + 8192 + 1024 * i) for i in range(2)]
    w6 = [place(f"w6_{i}", [128, 512], BF16, at=R + 12288 + 1024 * i) for i in range(8)]
    xld = [place(f"xld{i}", [128, 512], F32, at=R + 4096 + 2048 * i) for i in range(2)]
    outsb = [place(f"outsb{i}", [128, 512], F32, at=R + 8192 + 2048 * i) for i in range(2)]

    ps = [nc.alloc_psum_tensor(f"ps{i}", [128, 512], F32) for i in range(6)]
    psb_guard = [nc.psum_tensor(f"psb{i}", [128, 640], BF16) for i in range(2)]
    psb = [g.__enter__() for g in psb_guard]

    plan = Plan(nc)
    E, S, n = plan.emit, plan.sem, plan.n
    for s in (["s_ld", "xb", "cfl", "enl", "dbgl", "pe", "pe_s", "pe_pv",
               "ac", "scp", "dv", "dvm", "cc", "stc", "sto", "co_ld",
               "bb_ld", "cx_ld", "cst0", "cst1", "so0", "so1", "xl0", "xl1"]
              + [f"mtb{i}" for i in range(8)] + [f"xnw{i}" for i in range(3)]
              + [f"wv{i}" for i in range(3)] + [f"rop{i}" for i in range(2)]
              + [f"w6s{i}" for i in range(8)] + ["xb0", "xb1"]):
        S(s)

    E("sync", lambda e: e.dma_start(tri_sb[:], tri), inc="s_ld", inc_amt=16)
    E("sync", lambda e: e.dma_start(ones_sb[:], onesc), inc="s_ld", inc_amt=16)
    E("sync", lambda e: e.dma_start(id_sb[:], identb), inc="s_ld", inc_amt=16)
    E("sync", lambda e: e.dma_start(cofs_sb[:, 0:2], cofs), inc="co_ld", inc_amt=16)
    E("gpsimd", lambda e: e.dma_start(v01b[40:41, :], bbt), inc="bb_ld", inc_amt=16)

    xc_t = xc.rearrange("(t p) c -> t p c", p=128)
    xn_v = xn_fm.ap().rearrange("p (d t) -> p d t", t=TPC)
    p1_ts_dv = []
    p1_tr_pe = []
    p1_evac_scp = []
    for tt in range(NTT):
        par = tt % 2
        w = [("dv", p1_ts_dv[tt - 2])] if tt >= 2 else []
        E("sync", lambda e, tt=tt, par=par: e.dma_start(xbuf[par][:], xc_t[tt]),
          waits=w, inc=f"xb{par}", inc_amt=16)
        sld = (tt // 2 + 1) * 16
        for k in range(8):
            wk = [(f"xb{par}", sld)] if k == 0 else []
            E("vector", lambda e, k=k, par=par: e.bn_stats(
                bnbuf[:, 6 * k:6 * k + 6], xbuf[par][:, 512 * k:512 * k + 512]),
              waits=wk, inc="dv" if k == 7 else None)
        E("vector", lambda e: e.bn_aggr(stats[:, 0:2], bnbuf[:]),
          waits=[("dv", n("dv"))], inc="dv")
        E("vector", lambda e: e.tensor_single_scalar(
            stats[:, 4:5], stats[:, 1:2], 1e-5, ALU.add),
          waits=[("dv", n("dv"))], inc="dv")
        E("scalar", lambda e: e.activation(stats[:, 2:3], stats[:, 4:5],
                                           AF.Ln),
          waits=[("dv", n("dv"))], inc="scp")
        E("scalar", lambda e: e.activation(stats[:, 3:4], stats[:, 2:3],
                                           AF.Exp, scale=-0.5),
          waits=[("scp", n("scp"))], inc="scp")
        w = [("scp", n("scp"))]
        if tt >= 1:
            w.append(("pe", p1_tr_pe[tt - 1]))
        E("vector", lambda e, par=par: e.tensor_scalar(
            xnt[:], xbuf[par][:], stats[:, 0:1], stats[:, 3:4],
            ALU.subtract, ALU.mult), waits=w, inc="dv")
        p1_ts_dv.append(n("dv"))
        for g4 in range(8):
            pb = (tt * 8 + g4) % 2
            for j in range(4):
                dchunk = g4 * 4 + j
                w = []
                if j == 0:
                    w.append(("dv", p1_ts_dv[tt]))
                    if tt == 0 and g4 == 0:
                        w.append(("s_ld", 3 * 16))
                    if len(p1_evac_scp) >= 2:
                        w.append(("scp", p1_evac_scp[-2]))
                E("tensor", lambda e, pb=pb, j=j, dchunk=dchunk: e.transpose(
                    psb[pb][:, 128 * j:128 * j + 128],
                    xnt[:, 128 * dchunk:128 * dchunk + 128], id_sb[:]),
                  waits=w, inc="pe" if j == 3 else None)
            E("scalar", lambda e, pb=pb, g4=g4, tt=tt: e.copy(
                xn_v[:, g4 * 4:g4 * 4 + 4, tt * 128:tt * 128 + 128],
                psb[pb].ap()[:, 0:512].rearrange("p (j c) -> p j c", j=4)),
              waits=[("pe", n("pe"))], inc="scp")
            p1_evac_scp.append(n("scp"))
        p1_tr_pe.append(n("pe"))
    xn_done_scp = n("scp")

    cin_xn_t = cin_xn.rearrange("(d p) c -> d p c", p=128)
    for dchunk in range(NDC):
        E("scalar", lambda e, dchunk=dchunk: e.dma_start(
            cin_xn_t[dchunk], xn_fm[:, dchunk * TPC:(dchunk + 1) * TPC]),
          waits=[("scp", xn_done_scp)] if dchunk == 0 else [],
          inc="stc", inc_amt=16)
    E("gpsimd", lambda e: e.collective_compute(
        "AllGather", ALU.bypass, replica_groups=[list(range(NCORE))],
        ins=[cin_xn.opt()], outs=[cout_xn.opt()]),
      waits=[("stc", n("stc"))], inc="cc")

    if "d_xn" in dbg_out:
        for i in range(NDC * TPC // 512):
            E("gpsimd", lambda e, i=i: e.dma_start(
                dbg_out["d_xn"][:, i * 512:(i + 1) * 512],
                xn_fm[:, i * 512:(i + 1) * 512]),
              waits=[("scp", xn_done_scp)], inc="dbgl", inc_amt=16)

    if upto == "p1":
        with nc.Block() as blk:
            plan.build(blk)
        return nc

    p3_dv_acc = 0
    for ei in range(3):
        w = [] if ei == 0 else [("dv", p3_dv_acc)]
        E("gpsimd", lambda e, ei=ei: e.dma_start(cf_sb[:, 0:1024], c0r[ei]),
          waits=w, inc="cfl", inc_amt=16)
        E("gpsimd", lambda e, ei=ei: e.dma_start(cf_sb[:, 1024:1280], c1r[ei]),
          inc="cfl", inc_amt=16)
        E("gpsimd", lambda e, ei=ei: e.dma_start(mk_sb[:, 0:1024], m0r[ei]),
          inc="cfl", inc_amt=16)
        E("gpsimd", lambda e, ei=ei: e.dma_start(mk_sb[:, 1024:1280], m1r[ei]),
          inc="cfl", inc_amt=16)
        gl = n("cfl")
        if ei == 0:
            E("vector", lambda e: e.tensor_tensor(w01_f[:, 0:1280], cf_sb[:],
                                                  mk_sb[:], ALU.mult),
              waits=[("cfl", gl)], inc="dv")
        else:
            E("vector", lambda e: e.tensor_tensor(mk_sb[:], cf_sb[:], mk_sb[:],
                                                  ALU.mult),
              waits=[("cfl", gl)], inc="dv")
            E("vector", lambda e: e.tensor_tensor(w01_f[:, 0:1280],
                                                  w01_f[:, 0:1280], mk_sb[:],
                                                  ALU.add),
              waits=[("dv", n("dv"))], inc="dv")
        p3_dv_acc = n("dv")
    w01bv = w01_b.ap().rearrange("p (ch r) -> p ch r", r=40)
    E("vector", lambda e: e.tensor_copy(
        w01bv[:, :, 0:32],
        w01_f[:, 0:1024].rearrange("p (ch r) -> p ch r", r=32)),
      waits=[("dv", p3_dv_acc)], inc="dv")
    E("vector", lambda e: e.tensor_copy(
        w01bv[:, :, 32:40],
        w01_f[:, 1024:1280].rearrange("p (ch r) -> p ch r", r=8)),
      inc="dv")
    w01_ready = n("dv")
    E("gpsimd", lambda e: e.dma_start(enc_sb[:], enc01T.rearrange(
        "(ch p) r -> p ch r", p=128)), inc="enl", inc_amt=16)
    E("vector", lambda e: e.memset(e01b[32:64, :], 1.0), inc="dv")
    e01_ones_dv = n("dv")

    mt_r = mt.rearrange("(ch p) c -> ch p c", p=128)
    mt_grp = []
    v01_scp = []
    mload = 0
    mtb_cnt = [0] * 8
    for jb in range(8):
        for ch in range(NDC):
            ring = mload % 8
            w = []
            if mload >= 8:
                w.append(("pe", mt_grp[mload - 8]))
            E("gpsimd", lambda e, ch=ch, jb=jb, ring=ring: e.dma_start(
                mtb[ring][:], mt_r[ch][:, jb * 512:(jb + 1) * 512]),
              waits=w, inc=f"mtb{ring}", inc_amt=16)
            mload += 1
            mtb_cnt[ring] += 1
            w = [(f"mtb{ring}", mtb_cnt[ring] * 16)]
            if ch == 0:
                w.append(("dv", w01_ready))
                if jb >= 2:
                    w.append(("scp", v01_scp[jb - 2]))
            E("tensor", lambda e, ch=ch, jb=jb, ring=ring: e.matmul(
                ps[4 + jb % 2][0:40, :], w01_b[:, ch * 40:(ch + 1) * 40],
                mtb[ring][:], start=(ch == 0), stop=(ch == NDC - 1)),
              waits=w, inc="pe")
            mt_grp.append(n("pe"))
        E("scalar", lambda e, jb=jb: e.mul(
            v01b[0:40, jb * 512:(jb + 1) * 512], ps[4 + jb % 2][0:40, :], 0.5),
          waits=[("pe", n("pe"))], inc="scp")
        v01_scp.append(n("scp"))


    for tc in range(2):
        for ch in range(NDC):
            w = []
            if ch == 0:
                w = [("enl", 16), ("scp", v01_scp[tc + 6]),
                     ("scp", xn_done_scp)]
            E("tensor", lambda e, ch=ch, tc=tc: e.matmul(
                ps[4 + tc][0:40, :], enc_sb[:, ch * 40:(ch + 1) * 40],
                xn_fm[:, ch * TPC + tc * 512: ch * TPC + tc * 512 + 512],
                start=(ch == 0), stop=(ch == NDC - 1)),
              waits=w, inc="pe" if ch == NDC - 1 else None)
        E("scalar", lambda e, tc=tc: e.copy(
            e01b[0:40, tc * 512:(tc + 1) * 512], ps[4 + tc][0:40, :]),
          waits=[("pe", n("pe")), ("dv", e01_ones_dv)], inc="scp")
    p3b_done_scp = n("scp")
    p3b_done_pe = n("pe")

    if "d_w01" in dbg_out:
        E("gpsimd", lambda e: e.dma_start(dbg_out["d_w01"], w01_b[:]),
          waits=[("dv", w01_ready)], inc="dbgl", inc_amt=16)
    if "d_v01" in dbg_out:
        for i in range(8):
            E("gpsimd", lambda e, i=i: e.dma_start(
                dbg_out["d_v01"][0:64, i * 512:(i + 1) * 512],
                v01b[0:64, i * 512:(i + 1) * 512]),
              waits=[("scp", p3b_done_scp)], inc="dbgl", inc_amt=16)
    if "d_e01" in dbg_out:
        for i in range(2):
            E("gpsimd", lambda e, i=i: e.dma_start(
                dbg_out["d_e01"][0:64, i * 512:(i + 1) * 512],
                e01b[0:64, i * 512:(i + 1) * 512]),
              waits=[("scp", p3b_done_scp)], inc="dbgl", inc_amt=16)

    if upto == "p3b":
        with nc.Block() as blk:
            plan.build(blk)
        return nc

    cout_r = cout_xn.rearrange("(r d p) c -> r d p c", p=128, d=NDC)
    wq_r = wq_my.rearrange("(d p) c -> d p c", p=128)
    wkv_r = wkv_my.rearrange("(d p) c -> d p c", p=128)
    q_fm = regB.ap().rearrange("p (h t) -> p h t", h=NQH)

    grp_pe = []
    rop_cnt = [0, 0]
    xnw_cnt = [0] * 3
    wv_cnt = [0] * 3
    ld_i = 0
    q_dv = {}
    kv_dv = {}
    v_scp = {}
    evac_scp = []
    qtr_pe = {}
    ktr_pe = {}
    tab_tch_dv = []

    for tch in range(NTCH):
        r, sl = tch // 2, tch % 2
        tpar = tch % 2
        w = []
        if tch < 2:
            w.append(("pe", p3b_done_pe))
        else:
            w.append(("dv", tab_tch_dv[tch - 2]))
        E("sync", lambda e, tch=tch, tpar=tpar: e.dma_start(
            ropc[tpar][:], cosT[:, tch * 512:(tch + 1) * 512]),
          waits=w, inc=f"rop{tpar}", inc_amt=16)
        E("sync", lambda e, tch=tch, tpar=tpar: e.dma_start(
            rops[tpar][:], sinT[:, tch * 512:(tch + 1) * 512]),
          inc=f"rop{tpar}", inc_amt=16)
        rop_cnt[tpar] += 2
        tab_par, tab_here = tpar, rop_cnt[tpar] * 16

        for dp in range(16):
            ring = ld_i % 3
            w = []
            if ld_i == 0:
                w.append(("cc", 1))
            if ld_i < 3:
                w.append(("pe", p3b_done_pe))
            if ld_i >= 3:
                w.append(("pe", grp_pe[ld_i - 3]))
            E("sync", lambda e, r=r, sl=sl, dp=dp, ring=ring: e.dma_start(
                xnw[ring].ap().rearrange("p (d c) -> p d c", d=2),
                cout_r[r][dp * 2:dp * 2 + 2, :, sl * 512:(sl + 1) * 512]
                .rearrange("d p c -> p d c")),
              waits=w, inc=f"xnw{ring}", inc_amt=16)
            wg = [("pe", grp_pe[ld_i - 3] if ld_i >= 3 else p3b_done_pe)]
            E("gpsimd", lambda e, dp=dp, ring=ring: e.dma_start(
                wqw[ring].ap().rearrange("p (d c) -> p d c", d=2),
                wq_r[dp * 2:dp * 2 + 2, :, :].rearrange("d p c -> p d c")),
              waits=wg, inc=f"wv{ring}", inc_amt=16)
            E("gpsimd", lambda e, dp=dp, ring=ring: e.dma_start(
                wkw[ring].ap().rearrange("p (d c) -> p d c", d=2),
                wkv_r[dp * 2:dp * 2 + 2, :, :].rearrange("d p c -> p d c")),
              inc=f"wv{ring}", inc_amt=16)
            ld_i += 1
            xnw_cnt[ring] += 1
            wv_cnt[ring] += 2
            xw_here, w2_here = xnw_cnt[ring] * 16, wv_cnt[ring] * 16

            for dd in range(2):
                for tt in range(4):
                    gtile = tch * 4 + tt
                    w = []
                    if dd == 0 and tt == 0:
                        w = [(f"xnw{ring}", xw_here), (f"wv{ring}", w2_here)]
                    if dp == 0 and dd == 0 and (gtile - 4) in q_dv:
                        w.append(("dv", q_dv[gtile - 4]))
                    E("tensor", lambda e, dd=dd, tt=tt, ring=ring, dp=dp: e.matmul(
                        ps[tt][:], xnw[ring][:, dd * 512 + tt * 128: dd * 512 + tt * 128 + 128],
                        wqw[ring][:, dd * 512:(dd + 1) * 512],
                        start=(dp == 0 and dd == 0), stop=(dp == 15 and dd == 1),
                        skip_group_check=True), waits=w)
                for tt in range(4):
                    gtile = tch * 4 + tt
                    kvbk, kvc = 4 + tt // 2, (tt % 2) * 256
                    w = []
                    if dp == 0 and dd == 0 and tt % 2 == 0:
                        if tch == 0:
                            w.append(("scp", p3b_done_scp))
                        for gprev in (gtile - 4, gtile - 3):
                            if gprev in kv_dv:
                                w.append(("dv", kv_dv[gprev]))
                                w.append(("scp", v_scp[gprev]))
                    E("tensor", lambda e, dd=dd, tt=tt, ring=ring, kvbk=kvbk, kvc=kvc, dp=dp: e.matmul(
                        ps[kvbk][:, kvc:kvc + 256],
                        xnw[ring][:, dd * 512 + tt * 128: dd * 512 + tt * 128 + 128],
                        wkw[ring][:, dd * 256:(dd + 1) * 256],
                        start=(dp == 0 and dd == 0 and tt % 2 == 0),
                        stop=(dp == 15 and dd == 1),
                        skip_group_check=True),
                      waits=w, inc="pe" if (dd == 1 and tt == 3) else None)
            grp_pe.append(n("pe"))
        tch_mm_pe = n("pe")

        for tt in range(4):
            gtile = tch * 4 + tt
            spar = gtile % 2
            kvbk, kvc = 4 + tt // 2, (tt % 2) * 256
            tcol = tt * 128
            cosb = ropc[tpar].ap().rearrange(
                "p (a f) -> p a f", a=4)[:, tt:tt + 1, :].broadcast_to((128, 4, 128))
            sinb = rops[tpar].ap().rearrange(
                "p (a f) -> p a f", a=4)[:, tt:tt + 1, :].broadcast_to((128, 4, 128))
            cos2 = ropc[tpar].ap().rearrange(
                "p (a h f) -> p a h f", a=4, h=2)[:, tt:tt + 1, :, :].broadcast_to((128, 4, 2, 64))
            sin2 = rops[tpar].ap().rearrange(
                "p (a h f) -> p a h f", a=4, h=2)[:, tt:tt + 1, :, :].broadcast_to((128, 4, 2, 64))
            qv = ps[tt].ap().rearrange("p (b h f) -> p b h f", b=4, h=2)
            s2v = scrq2.ap().rearrange("p (b h f) -> p b h f", b=4, h=2)
            w = [("pe", tch_mm_pe), (f"rop{tab_par}", tab_here)]
            E("vector", lambda e, qv=qv, s2v=s2v, sin2=sin2: e.tensor_tensor(
                s2v[:, :, 0, :], qv[:, :, 1, :], sin2[:, :, 0, :], ALU.mult),
              waits=w, inc="dv")
            E("vector", lambda e, qv=qv, s2v=s2v, sin2=sin2: e.tensor_tensor(
                s2v[:, :, 1, :], qv[:, :, 0, :], sin2[:, :, 1, :], ALU.mult),
              inc="dv")
            E("vector", lambda e, tt=tt, cosb=cosb: e.tensor_tensor(
                scrq.ap().rearrange("p (b f) -> p b f", b=4),
                ps[tt].ap().rearrange("p (b f) -> p b f", b=4), cosb, ALU.mult),
              inc="dv")
            q_dv[gtile] = n("dv")
            w = [("dv", n("dv"))]
            if spar in qtr_pe:
                w.append(("pe", qtr_pe[spar]))
            E("vector", lambda e, spar=spar: e.tensor_tensor(
                scrb[spar][:], scrq[:], scrq2[:], ALU.add), waits=w, inc="dv")
            scrb_dv = n("dv")
            kv2 = ps[kvbk].ap()[:, kvc:kvc + 128].rearrange(
                "p (h f) -> p h f", h=2)
            s2k = scrq2.ap()[:, 0:128].rearrange("p (h f) -> p h f", h=2)
            c1b = ropc[tpar].ap().rearrange(
                "p (a h f) -> p a h f", a=4, h=2)[:, tt, :, :]
            s1b = rops[tpar].ap().rearrange(
                "p (a h f) -> p a h f", a=4, h=2)[:, tt, :, :]
            E("vector", lambda e, kv2=kv2, s2k=s2k, s1b=s1b: e.tensor_tensor(
                s2k[:, 0, :], kv2[:, 1, :], s1b[:, 0, :], ALU.mult),
              waits=[("dv", scrb_dv)], inc="dv")
            E("vector", lambda e, kv2=kv2, s2k=s2k, s1b=s1b: e.tensor_tensor(
                s2k[:, 1, :], kv2[:, 0, :], s1b[:, 1, :], ALU.mult), inc="dv")
            E("vector", lambda e, kvbk=kvbk, kvc=kvc, c1b=c1b: e.tensor_tensor(
                scrq.ap()[:, 0:128].rearrange("p (h f) -> p h f", h=2),
                ps[kvbk].ap()[:, kvc:kvc + 128].rearrange("p (h f) -> p h f", h=2),
                c1b, ALU.mult), inc="dv")
            kv_dv[gtile] = n("dv")
            if tt == 3:
                tab_tch_dv.append(n("dv"))
            w = [("dv", n("dv"))]
            if spar in ktr_pe:
                w.append(("pe", ktr_pe[spar]))
            E("vector", lambda e, spar=spar: e.tensor_tensor(
                kscr[spar][:], scrq[:, 0:128], scrq2[:, 0:128], ALU.add),
              waits=w, inc="dv")
            kscr_dv = n("dv")
            E("scalar", lambda e, kvbk=kvbk, kvc=kvc, gtile=gtile: e.copy(
                v_res[:, gtile * 128:(gtile + 1) * 128],
                ps[kvbk][:, kvc + 128:kvc + 256]),
              waits=[("pe", tch_mm_pe)], inc="scp")
            v_scp[gtile] = n("scp")
            for h in range(NQH):
                w = []
                if h == 0:
                    w = [("dv", scrb_dv)]
                    if len(evac_scp) >= 2:
                        w.append(("scp", evac_scp[-2]))
                E("tensor", lambda e, h=h, spar=spar: e.transpose(
                    psb[spar][:, h * 128:(h + 1) * 128],
                    scrb[spar][:, h * 128:(h + 1) * 128], id_sb[:]),
                  waits=w, inc="pe" if h == 3 else None)
            qtr_pe[spar] = n("pe")
            E("tensor", lambda e, spar=spar: e.transpose(
                psb[spar][:, 512:640], kscr[spar][:], id_sb[:]),
              waits=[("dv", kscr_dv)], inc="pe")
            ktr_pe[spar] = n("pe")
            E("scalar", lambda e, spar=spar, gtile=gtile: e.copy(
                q_fm[:, :, gtile * 128:(gtile + 1) * 128],
                psb[spar].ap()[:, 0:512].rearrange("p (h c) -> p h c", h=4)),
              waits=[("pe", n("pe"))], inc="scp")
            E("scalar", lambda e, spar=spar, gtile=gtile: e.copy(
                k_res[:, gtile * 128:(gtile + 1) * 128], psb[spar][:, 512:640]),
              inc="scp")
            evac_scp.append(n("scp"))
    p2_done_scp = n("scp")

    for nm, buf, wd in (("d_q", regB, NQH * T), ("d_k", k_res, T), ("d_v", v_res, T)):
        if nm in dbg_out:
            for i in range(wd // 512):
                E("gpsimd", lambda e, i=i, nm=nm, buf=buf: e.dma_start(
                    dbg_out[nm][:, i * 512:(i + 1) * 512],
                    buf[:, i * 512:(i + 1) * 512]),
                  waits=[("scp", p2_done_scp)], inc="dbgl", inc_amt=16)

    if upto == "p2":
        with nc.Block() as blk:
            plan.build(blk)
        return nc

    for g in reversed(psb_guard):
        g.__exit__(None, None, None)
    ps6 = [nc.alloc_psum_tensor(f"ps6_{i}", [128, 512], F32) for i in range(2)]
    p6b = ps + ps6
    units = []
    gid = 0
    for hq in range(NQH):
        for qch in range(NTCH):
            nkt = (qch + 1) * 4
            for kt in range(nkt):
                diag = kt >= qch * 4
                qoff = (kt - qch * 4) * 128 if diag else 0
                units.append((hq, qch, kt, diag, qoff, 512 - qoff,
                              hq * T + qch * 512 + qoff, gid, kt == nkt - 1))
            gid += 1
    NU = len(units)
    ctx_of = lambda g: (ps6[0] if g % 2 else ps[4])
    rs_of = lambda g: (ps6[1] if g % 2 else ps[5])

    mask_cnt = [0]
    mask_idx = {}
    ev_dv = []
    grp_pv = {}
    cst_cnt = [0, 0]

    def emit_S(u):
        hq, qch, kt, diag, qoff, N, qcol, g, last = units[u]
        w = []
        if u == 0:
            w.append(("scp", p2_done_scp))
        if u >= 4:
            w.append(("ac", u - 3))
        E("tensor", lambda e, kt=kt, qcol=qcol, qoff=qoff, N=N, sbk=u % 4: e.matmul(
            ps[sbk][:, qoff:qoff + N], k_res[:, kt * 128:(kt + 1) * 128],
            regB[:, qcol:qcol + N], start=True, stop=True),
          waits=w, inc="pe_s")

    def emit_exp(u):
        hq, qch, kt, diag, qoff, N, qcol, g, last = units[u]
        w = [("pe_s", u + 1)]
        if u >= 4:
            w.append(("pe_pv", u - 3))
        E("scalar", lambda e, sbk=u % 4, qoff=qoff, N=N, u=u: e.activation(
            pt_sb[u % 4][:, qoff:qoff + N], ps[sbk][:, qoff:qoff + N],
            AF.Exp, scale=SCALE),
          waits=w, inc="ac")
        if diag:
            E("vector", lambda e, u=u, qoff=qoff: e.tensor_tensor(
                pt_sb[u % 4][:, qoff:qoff + 128],
                pt_sb[u % 4][:, qoff:qoff + 128], tri_sb[:], ALU.mult),
              waits=[("ac", u + 1)], inc="dvm")
            mask_cnt[0] += 1
            mask_idx[u] = mask_cnt[0]

    def emit_rspv(u):
        hq, qch, kt, diag, qoff, N, qcol, g, last = units[u]
        w = [("ac", u + 1)]
        if diag:
            w.append(("dvm", mask_idx[u]))
        if kt == 0 and g >= 2:
            w.append(("dv", ev_dv[g - 2]))
        E("tensor", lambda e, u=u, qoff=qoff, N=N, kt=kt, g=g, last=last: e.matmul(
            rs_of(g)[:, qoff:qoff + N], ones_sb[:],
            pt_sb[u % 4][:, qoff:qoff + N],
            start=(kt == 0), stop=last, skip_group_check=True), waits=w)
        E("tensor", lambda e, u=u, qoff=qoff, N=N, kt=kt, g=g, last=last: e.matmul(
            ctx_of(g)[:, qoff:qoff + N], v_res[:, kt * 128:(kt + 1) * 128],
            pt_sb[u % 4][:, qoff:qoff + N],
            start=(kt == 0), stop=last, skip_group_check=True), inc="pe_pv")
        if last:
            grp_pv[g] = n("pe_pv")
            emit_evac(g, hq, qch)

    def emit_evac(g, hq, qch):
        epar = g % 2
        w = [("pe_pv", grp_pv[g])]
        if g >= 2:
            w.append(("dv", ev_dv[g - 2]))
        E("vector", lambda e, epar=epar, g=g: e.reciprocal(
            rsr_sb[epar][:], rs_of(g)[:]), waits=w, inc="dv")
        w = [("dv", n("dv"))]
        if g >= 2:
            w.append((f"cst{epar}", cst_cnt[epar] * 16))
        E("vector", lambda e, epar=epar, g=g: e.tensor_tensor(
            ctxb[epar][:], ctx_of(g)[:], rsr_sb[epar][:], ALU.mult),
          waits=w, inc="dv")
        ev_dv.append(n("dv"))
        E("scalar", lambda e, epar=epar, hq=hq, qch=qch: e.dma_start(
            cin_c[hq * 128:(hq + 1) * 128, qch * 512:(qch + 1) * 512],
            ctxb[epar][:]),
          waits=[("dv", n("dv"))], inc=f"cst{epar}", inc_amt=16)
        cst_cnt[epar] += 1

    for u in range(NU):
        emit_S(u)
        emit_exp(u)
        if u >= 3:
            emit_rspv(u - 3)
    emit_rspv(NU - 3)
    emit_rspv(NU - 2)
    emit_rspv(NU - 1)
    p4_ac = n("ac")
    p4_pv = n("pe_pv")
    p4_dv_last = ev_dv[-1]

    wb_r = wbT.rearrange("(d p) c -> d p c", p=128)
    w6_cnt = [0] * 8
    for d in range(8):
        E("gpsimd", lambda e, d=d: e.dma_start(
            w6[d][:], wb_r[d][:, 0:512]),
          waits=[("scp", p2_done_scp)] if d == 0 else [],
          inc=f"w6s{d}", inc_amt=16)
        w6_cnt[d] += 1

    E("gpsimd", lambda e: e.collective_compute(
        "AllGather", ALU.bypass, replica_groups=[list(range(NCORE))],
        ins=[cin_c.opt()], outs=[cout_c.opt()]),
      waits=[("cst0", cst_cnt[0] * 16), ("cst1", cst_cnt[1] * 16)], inc="cc")

    if "d_ctx" in dbg_out:
        for i in range(4):
            E("gpsimd", lambda e, i=i: e.dma_start(
                dbg_out["d_ctx"][i * 128:(i + 1) * 128, :],
                cin_c[i * 128:(i + 1) * 128, :]),
              waits=[("cc", 2)], inc="dbgl", inc_amt=16)

    if upto == "p4":
        with nc.Block() as blk:
            plan.build(blk)
        return nc


    cout_c_p = cout_c.rearrange("(d p) c -> p d c", p=128)

    def load_ctx_branched(e):
        last = {}
        with e.register("cid") as cid:
            e.reg_load(cid, cofs_sb[0:1, 1:2])
            for k in range(NCORE):
                with e.If_eq(cid, k):
                    e.dma_start(
                        regB.ap().rearrange("p (d t) -> p d t", t=TPC),
                        cout_c_p[:, :, k * TPC:(k + 1) * TPC]
                    ).then_inc(plan.handles["cx_ld"], 16)
            e.end_ifs()
        return None

    S("cx_ld")
    plan.counts["cx_ld"] += 16
    plan.items["sync"].append((load_ctx_branched,
                               [(S("cc"), 2), (S("co_ld"), 16)], None, 1))
    ctx_ld = n("cx_ld")

    wo_r = wo.rearrange("(d p) c -> d p c", p=128)
    outv = out.rearrange("(t p) c -> t p c", p=128)

    wgrp_pe = []
    w6i = 0
    p6_ev_dv = []
    nst = 0
    xl_cnt = [0, 0]
    so_cnt = [0, 0]
    for ob in range(8):
        for si, (wsrc, lhs) in enumerate(((wb_r, xn_fm), (wo_r, regB))):
            for d in range(NDC):
                ring = w6i % 8
                if w6i >= 8:
                    w = []
                    if w6i >= 8:
                        w.append(("pe", wgrp_pe[w6i - 8]))
                    E("gpsimd", lambda e, d=d, ob=ob, ring=ring, wsrc=wsrc: e.dma_start(
                        w6[ring][:], wsrc[d][:, ob * 512:(ob + 1) * 512]),
                      waits=w, inc=f"w6s{ring}", inc_amt=16)
                    w6_cnt[ring] += 1
                w6i += 1
                wld = w6_cnt[ring] * 16
                for tt in range(NTT):
                    w = []
                    if tt == 0:
                        w = [(f"w6s{ring}", wld)]
                    if si == 0 and d == 0:
                        if ob == 0:
                            if tt == 0:
                                w.append(("ac", p4_ac))
                                w.append(("scp", p3b_done_scp))
                                w.append(("bb_ld", 16))
                            if tt == 4:
                                w.append(("dv", p4_dv_last))
                        else:
                            w.append(("dv", p6_ev_dv[(ob - 1) * 8 + tt]))
                    if si == 1 and d == 0 and tt == 0 and ob == 0:
                        w.append(("cx_ld", ctx_ld))
                    first = (si == 0 and d == 0)
                    E("tensor", lambda e, d=d, tt=tt, ring=ring, lhs=lhs, first=first: e.matmul(
                        p6b[tt][:],
                        lhs[:, d * TPC + tt * 128: d * TPC + tt * 128 + 128],
                        w6[ring][:], start=first, stop=False,
                        skip_group_check=True),
                      waits=w, inc="pe" if tt == NTT - 1 else None)
                wgrp_pe.append(n("pe"))
        for tt in range(NTT):
            E("tensor", lambda e, tt=tt, ob=ob: e.matmul(
                p6b[tt][:], e01b[0:41, tt * 128:(tt + 1) * 128],
                v01b[0:41, ob * 512:(ob + 1) * 512],
                start=False, stop=True, skip_group_check=True),
              inc="pe_pv")
            pv_here = n("pe_pv")
            xpar = (ob * NTT + tt) % 2
            w = []
            if len(p6_ev_dv) >= 2:
                w.append(("dv", p6_ev_dv[-2]))
            elif ob == 0 and tt < 2:
                w.append(("dv", p4_dv_last))
            E("sync", lambda e, ob=ob, tt=tt, xpar=xpar: e.dma_start(
                xld[xpar][:], xc_t[tt][:, ob * 512:(ob + 1) * 512]),
              waits=w, inc=f"xl{xpar}", inc_amt=16)
            xl_cnt[xpar] += 1
            w = [("pe_pv", pv_here), (f"xl{xpar}", xl_cnt[xpar] * 16)]
            if nst >= 2:
                w.append((f"so{nst % 2}", so_cnt[nst % 2] * 16))
            elif nst < 2:
                w.append((f"cst{nst % 2}", cst_cnt[nst % 2] * 16))
            E("vector", lambda e, tt=tt, xpar=xpar, opar=nst % 2: e.tensor_tensor(
                outsb[opar][:], p6b[tt][:], xld[xpar][:], ALU.add),
              waits=w, inc="dv")
            p6_ev_dv.append(n("dv"))
            E("scalar", lambda e, tt=tt, ob=ob, opar=nst % 2: e.dma_start(
                outv[tt][:, ob * 512:(ob + 1) * 512], outsb[opar][:]),
              waits=[("dv", n("dv"))], inc=f"so{nst % 2}", inc_amt=16)
            so_cnt[nst % 2] += 1
            nst += 1

    E("sync", lambda e: None, waits=[("so0", so_cnt[0] * 16),
                                     ("so1", so_cnt[1] * 16)])
    E("gpsimd", lambda e: None, waits=[("dbgl", n("dbgl"))])

    with nc.Block() as blk:
        plan.build(blk)
    return nc



def kernel(L0_attn_resid, L0_attn_Wq, L0_attn_Wk, L0_attn_Wv, L0_attn_Wo,
           L0_ffn_W_base, L0_ffn_b_base, L0_ffn_enc0, L0_ffn_enc1,
           L0_ffn_coefs0, L0_ffn_coefs1, L0_ffn_mask0, L0_ffn_mask1):
    x = np.ascontiguousarray(np.asarray(L0_attn_resid, dtype=np.float32)[0])
    Wq = np.asarray(L0_attn_Wq, dtype=np.float32)
    Wk = np.asarray(L0_attn_Wk, dtype=np.float32)
    Wv = np.asarray(L0_attn_Wv, dtype=np.float32)
    Wo = np.ascontiguousarray(np.asarray(L0_attn_Wo, dtype=np.float32))
    WbT = np.ascontiguousarray(np.asarray(L0_ffn_W_base, dtype=np.float32).T)
    bbv = np.ascontiguousarray(np.asarray(L0_ffn_b_base, np.float32)[None, :])
    enc01T = np.ascontiguousarray(
        np.concatenate([np.asarray(L0_ffn_enc0, np.float32).T,
                        np.asarray(L0_ffn_enc1, np.float32).T], axis=1))
    c0 = np.ascontiguousarray(
        np.asarray(L0_ffn_coefs0, np.float32).reshape(3, 32, 128, 32)
        .transpose(0, 2, 1, 3).reshape(3, 128, 1024))
    c1 = np.ascontiguousarray(
        np.asarray(L0_ffn_coefs1, np.float32).reshape(3, 32, 128, 8)
        .transpose(0, 2, 1, 3).reshape(3, 128, 256))
    m0 = np.ascontiguousarray(
        np.asarray(L0_ffn_mask0).astype(np.uint8).reshape(3, 32, 128, 32)
        .transpose(0, 2, 1, 3).reshape(3, 128, 1024))
    m1 = np.ascontiguousarray(
        np.asarray(L0_ffn_mask1).astype(np.uint8).reshape(3, 32, 128, 8)
        .transpose(0, 2, 1, 3).reshape(3, 128, 256))
    MT = build_MT().astype(BFNP)

    cosv, sinv = rope_tables()
    cos_t = np.empty((128, GT * 128), np.float32)
    sin_t = np.empty((128, GT * 128), np.float32)
    for g in range(GT):
        cb = cosv[g * 128:(g + 1) * 128]
        sb = sinv[g * 128:(g + 1) * 128]
        cos_t[:, g * 128:g * 128 + 64] = cb
        cos_t[:, g * 128 + 64:g * 128 + 128] = cb
        sin_t[:, g * 128:g * 128 + 64] = -sb
        sin_t[:, g * 128 + 64:g * 128 + 128] = sb

    triv = np.triu(np.ones((128, 128), np.float32)).astype(BFNP)
    onesv = np.ones((128, 128), np.float32).astype(BFNP)
    identv = np.eye(128, dtype=np.float32).astype(BFNP)

    nc = build_program()
    in_maps = []
    for c in range(NCORE):
        in_maps.append({
            "xc": np.ascontiguousarray(x[c * TPC:(c + 1) * TPC]),
            "wq_my": np.ascontiguousarray(Wq[:, c * 512:(c + 1) * 512]),
            "wkv_my": np.ascontiguousarray(
                np.concatenate([Wk[:, c * 128:(c + 1) * 128],
                                Wv[:, c * 128:(c + 1) * 128]], axis=1)),
            "wo": Wo, "wbT": WbT, "mt": MT, "enc01T": enc01T,
            "c0r": c0, "c1r": c1, "m0r": m0, "m1r": m1, "bb": bbv,
            "cosT": cos_t, "sinT": sin_t,
            "tri": triv, "onesc": onesv, "identb": identv,
            "cofs": np.array([[c * TPC, c]], np.int32),
        })
    res = run_bass_kernel_spmd(nc, in_maps, core_ids=list(range(NCORE)))
    pieces = [res.results[c]["out"] for c in range(NCORE)]
    return np.concatenate(pieces, axis=0)[None, :, :]
